# version 08
# baseline (speedup 1.0000x reference)
import os
os.environ.setdefault("BASS_INTERP_NUM_WORKERS", "8")

import contextlib
import numpy as np
import ml_dtypes

import concourse.bass as bass
import concourse.mybir as mybir
from concourse.bass_utils import run_bass_kernel_spmd

F32 = mybir.dt.float32
BF16 = mybir.dt.bfloat16
ACT = mybir.ActivationFunctionType
ALU = mybir.AluOpType
AX = mybir.AxisListType.X

NCORES = 8
S = 16384
D = 1024
T = S // NCORES
PD = D // 128
F = 512
E = 64
TOPK = 8
EPS = 1e-5
BE = 4
NTB = S // 512
SERIAL = bool(int(os.environ.get("KERNEL_SERIAL", "0")))
NQT = T // 128
NMC = S // 2048

ENGINES = ("sync", "gpsimd", "vector", "scalar", "tensor")


class Sem:
    __slots__ = ("name", "n", "h")

    def __init__(self, name):
        self.name = name
        self.n = 0
        self.h = None


class Prog:
    def __init__(self, nc):
        self.nc = nc
        self.ops = {e: [] for e in ENGINES}
        self.sems = []

    def sem(self, name):
        s = Sem(name)
        self.sems.append(s)
        return s

    def wait(self, eng, sem, tgt=None):
        t = sem.n if tgt is None else tgt
        if t > 0:
            self.ops[eng].append(lambda En, s=sem, t=t: En.wait_ge(s.h, t))

    def dma(self, eng, sem, out, in_):
        aser = self.auto_serial.get(eng)
        if aser is not None:
            self.wait(eng, aser)
        sem.n += 16
        self.ops[eng].append(
            lambda En, o=out, i=in_, s=sem: En.dma_start(out=o, in_=i).then_inc(s.h, 16))
        return sem.n

    auto_serial = {}

    def op(self, eng, f, sem=None):
        aser = self.auto_serial.get(eng)
        if aser is not None:
            if sem is None:
                sem = aser
            self.wait(eng, aser)
        if sem is not None:
            sem.n += 1

            def g(En, f=f, s=sem):
                f(En).then_inc(s.h, 1)
        else:
            def g(En, f=f):
                f(En)
        self.ops[eng].append(g)
        return sem.n if sem is not None else None

    def barrier(self, sems):
        for eng in ENGINES:
            for s in sems:
                self.wait(eng, s)

    def finalize(self):
        nc = self.nc
        with contextlib.ExitStack() as es:
            for s in self.sems:
                s.h = es.enter_context(nc.semaphore(s.name))
            with nc.Block() as block:
                block.sync(lambda En: [f(En) for f in self.ops["sync"]])
                block.gpsimd(lambda En: [f(En) for f in self.ops["gpsimd"]])
                block.vector(lambda En: [f(En) for f in self.ops["vector"]])
                block.scalar(lambda En: [f(En) for f in self.ops["scalar"]])
                block.tensor(lambda En: [f(En) for f in self.ops["tensor"]])


def build_nc(debug=()):
    nc = bass.Bass("TRN2", target_bir_lowering=False,
                   detect_race_conditions=False)
    p = Prog(nc)

    x = nc.dram_tensor("x", [S, D], F32, kind="ExternalInput").ap()
    xq = nc.dram_tensor("xq", [T, D], F32, kind="ExternalInput").ap()
    xmoe = nc.dram_tensor("xmoe", [T, D], F32, kind="ExternalInput").ap()
    wq = nc.dram_tensor("wq", [D, D], F32, kind="ExternalInput").ap()
    wk = nc.dram_tensor("wk", [D, D], F32, kind="ExternalInput").ap()
    wv = nc.dram_tensor("wv", [D, D], F32, kind="ExternalInput").ap()
    wo = nc.dram_tensor("wo", [D, D], F32, kind="ExternalInput").ap()
    cent = nc.dram_tensor("cent", [E, D], F32, kind="ExternalInput").ap()
    ebias = nc.dram_tensor("ebias", [1, E], F32, kind="ExternalInput").ap()
    wsup = nc.dram_tensor("wsup", [D, F], F32, kind="ExternalInput").ap()
    wsdn = nc.dram_tensor("wsdn", [F, D], F32, kind="ExternalInput").ap()
    wup = nc.dram_tensor("wup", [E, D, F], F32, kind="ExternalInput").ap()
    wdn = nc.dram_tensor("wdn", [E, F, D], F32, kind="ExternalInput").ap()
    y = nc.dram_tensor("y", [T, D], F32, kind="ExternalOutput").ap()
    ident_b_h = nc.dram_tensor("ident_b_h", [128, 128], BF16,
                               kind="ExternalInput").ap()
    ident_f_h = nc.dram_tensor("ident_f_h", [128, 128], F32,
                               kind="ExternalInput").ap()
    ones_b_h = nc.dram_tensor("ones_b_h", [1, 128], BF16,
                              kind="ExternalInput").ap()
    ones_f_h = nc.dram_tensor("ones_f_h", [1, 128], F32,
                              kind="ExternalInput").ap()
    cpool_h = nc.dram_tensor("cpool_h", [128, 8], F32, kind="ExternalInput").ap()

    kt_h = nc.dram_tensor("kt_h", [PD, 128, S], BF16, kind="Internal").ap()
    v_h = nc.dram_tensor("v_h", [S // 128, 128, D], BF16, kind="Internal").ap()
    qt_h = nc.dram_tensor("qt_h", [PD, 128, T], BF16, kind="Internal").ap()
    ffn_h = nc.dram_tensor("ffn_h", [PD, 128, T], F32, kind="Internal").ap()
    xmT_h = nc.dram_tensor("xmT_h", [PD, 128, T], F32, kind="Internal").ap()
    out_h = nc.dram_tensor("out_h", [NQT, 128, D], BF16, kind="Internal").ap()

    dbg = {}
    for name in debug:
        shp = {"dbg_kt": [PD, 128, S], "dbg_v": [S // 128, 128, D],
               "dbg_qt": [PD, 128, T], "dbg_ffn": [PD, 128, T],
               "dbg_xmT": [PD, 128, T], "dbg_out": [NQT, 128, D],
               "dbg_cwt": [E, T], "dbg_aff": [E, T]}[name]
        dt = BF16 if name in ("dbg_kt", "dbg_v", "dbg_qt", "dbg_out") else F32
        dbg[name] = nc.dram_tensor(name, shp, dt, kind="ExternalOutput").ap()

    sLd = p.sem("sLd")
    sMisc = p.sem("sMisc")
    sWld = [p.sem("sWld0"), p.sem("sWld1")]
    sXb = [p.sem("sXb0"), p.sem("sXb1")]
    sWst = p.sem("sWst")
    sCwr = p.sem("sCwr")
    sFs = [p.sem(f"sFs{i}") for i in range(4)]
    sFst = [p.sem(f"sFst{i}") for i in range(4)]
    sKV = [p.sem("sKV0"), p.sem("sKV1")]
    sQt = [p.sem("sQt0"), p.sem("sQt1")]
    sY = [p.sem("sY0"), p.sem("sY1")]
    sRs = p.sem("sRs")
    sSt = p.sem("sSt")
    sSt2 = p.sem("sSt2")
    sT = p.sem("sT")
    sV = p.sem("sV")
    sS = p.sem("sS")
    sG = p.sem("sG")
    ALL_SEMS = ([sLd, sMisc, sWst, sCwr, sSt, sSt2, sT, sV, sS, sG]
                + sWld + sXb + sFs + sFst + sKV + sQt + sY + [sRs])
    p.auto_serial = {"vector": sV, "scalar": sS}

    st_glob = contextlib.ExitStack()
    sbg = lambda name, shape, dt: st_glob.enter_context(nc.sbuf_tensor(name, shape, dt))

    ident_b = sbg("ident_b", [128, 128], BF16)
    ident_f = sbg("ident_f", [128, 128], F32)
    ones_b = sbg("ones_b", [1, 128], BF16)
    ones_f = sbg("ones_f", [1, 128], F32)

    cpool = sbg("cpool", [128, 8], F32)
    p.dma("sync", sMisc, ident_b[:, :], ident_b_h[:, :])
    p.dma("sync", sMisc, ident_f[:, :], ident_f_h[:, :])
    p.dma("sync", sMisc, ones_b[:, :], ones_b_h[:, :])
    p.dma("sync", sMisc, ones_f[:, :], ones_f_h[:, :])
    p.dma("sync", sMisc, cpool[:, :], cpool_h[:, :])
    CONSTS = [EPS, 0.125, 1.0, 1e-8, 1.0 / D, -1.0, -1e30, 0.0]
    for ci_, cv_ in enumerate(CONSTS):
        nc.const_aps.register(F32, cv_, cpool[:, ci_:ci_ + 1])
    C_EPS, C_QS, C_ONE, C_E8, C_RD, C_NEG, C_BIG, C_ZERO = (
        cpool[:, i:i + 1] for i in range(8))
    CONSTS_LD = sMisc.n
    p.wait("tensor", sMisc, CONSTS_LD)
    p.wait("scalar", sMisc, CONSTS_LD)
    p.wait("vector", sMisc, CONSTS_LD)
    p.wait("gpsimd", sMisc, CONSTS_LD)

    wld_carry = {0: 0, 1: 0}

    def load_wT(w_ap, wT_sb, wld, psA, psB):
        tr_done = {}
        for m in range(PD):
            p.wait("sync", sT, wld_carry[m % 2])
            t_ld = p.dma("sync", sWld[m % 2], wld[:, m % 2, :],
                         w_ap[m * 128:(m + 1) * 128, :])
            p.wait("tensor", sWld[m % 2], t_ld)
            ps = psA if (m % 2 == 0) else psB
            for c in range(PD):
                if c % 4 == 0:
                    p.wait("tensor", sV, sV.n)
                p.op("tensor", lambda En, ps=ps, c=c, m=m: En.transpose(
                    out=ps[:, (c % 4) * 128:(c % 4 + 1) * 128],
                    in_=wld[:, m % 2, c * 128:(c + 1) * 128],
                    identity=ident_f[:, :]),
                    sem=sT if c % 4 == 3 else None)
                if c % 4 == 3:
                    tt = sT.n
                    p.wait("vector", sT, tt)
                    for cc in range(c - 3, c + 1):
                        p.op("vector", lambda En, ps=ps, cc=cc, m=m: En.tensor_copy(
                            out=wT_sb[:, cc * PD + m, :],
                            in_=ps[:, (cc % 4) * 128:(cc % 4 + 1) * 128]),
                            sem=sV)
            tr_done[m] = wld_carry[m % 2] = sT.n

    def emit_ln_stats(xin_g, G, stats, sqb):
        stats = stats[:, :]
        v, s = "vector", "scalar"
        p.op(v, lambda En: En.reduce_sum(out=stats[:, 0:G], in_=xin_g, axis=AX))
        p.op(v, lambda En: En.tensor_scalar_mul(out=stats[:, 0:G],
                                                in0=stats[:, 0:G], scalar1=C_RD))
        for h in range(0, G, 4):
            w = min(4, G - h)
            sqv = sqb[:, 0:w * D].rearrange("p (g d) -> p g d", g=w)
            if h > 0:
                p.wait(s, sV, sV.n)
            p.op(s, lambda En, h=h, w=w: En.activation(
                out=sqb[:, 0:w * D],
                in_=xin_g[:, h:h + w, :].rearrange("p g d -> p (g d)"),
                func=ACT.Square), sem=sS)
            p.wait(v, sS)
            p.op(v, lambda En, h=h, w=w, sqv=sqv: En.reduce_sum(
                out=stats[:, G + h:G + h + w], in_=sqv, axis=AX), sem=sV)
        p.op(v, lambda En: En.tensor_scalar_mul(out=stats[:, G:2 * G],
                                                in0=stats[:, G:2 * G], scalar1=C_RD))
        p.op(v, lambda En: En.tensor_mul(out=stats[:, 4 * G:5 * G],
                                         in0=stats[:, 0:G], in1=stats[:, 0:G]))
        p.op(v, lambda En: En.tensor_sub(out=stats[:, G:2 * G],
                                         in0=stats[:, G:2 * G],
                                         in1=stats[:, 4 * G:5 * G]), sem=sV)
        p.wait(s, sV)
        p.op(s, lambda En: En.activation(out=stats[:, G:2 * G],
                                         in_=stats[:, G:2 * G],
                                         func=ACT.Sqrt, bias=EPS), sem=sS)
        p.wait(v, sS)
        p.op(v, lambda En: En.reciprocal(out=stats[:, 2 * G:3 * G],
                                         in_=stats[:, G:2 * G]), sem=sV)
        p.wait(v, sV)
        p.op(v, lambda En: En.tensor_mul(out=stats[:, 3 * G:4 * G],
                                         in0=stats[:, 0:G],
                                         in1=stats[:, 2 * G:3 * G]))
        p.op(v, lambda En: En.tensor_scalar_mul(out=stats[:, 3 * G:4 * G],
                                                in0=stats[:, 3 * G:4 * G],
                                                scalar1=C_NEG), sem=sV)

    st_p1 = contextlib.ExitStack()
    sb1 = lambda name, shape, dt: st_p1.enter_context(nc.sbuf_tensor(name, shape, dt))
    ps1 = contextlib.ExitStack()
    psb1 = lambda name, shape: ps1.enter_context(nc.psum_tensor(name, shape, F32))

    wqT = sb1("wqT", [128, PD * PD, 128], BF16)
    wkT = sb1("wkT", [128, PD * PD, 128], BF16)
    wvT = sb1("wvT", [128, PD * PD, 128], BF16)
    psA = psb1("psA", [128, 512])
    psB = psb1("psB", [128, 512])
    psC = psb1("psC", [128, 512])
    psDq = psb1("psDq", [128, 512])
    psAb = ps1.enter_context(nc.psum_tensor("psAb", [128, 512], BF16))
    psBb = ps1.enter_context(nc.psum_tensor("psBb", [128, 512], BF16))
    psCb = ps1.enter_context(nc.psum_tensor("psCb", [128, 512], BF16))
    psDb = ps1.enter_context(nc.psum_tensor("psDb", [128, 512], BF16))

    st_wld = contextlib.ExitStack()
    wld = st_wld.enter_context(nc.sbuf_tensor("wld", [128, 2, D], F32))
    load_wT(wq, wqT, wld, psA, psB)
    load_wT(wk, wkT, wld, psA, psB)
    load_wT(wv, wvT, wld, psA, psB)
    st_wld.close()
    p.wait("sync", sT, sT.n)

    xb = sb1("xb", [128, 2, 8, D], F32)
    sqb = sb1("sqb", [128, 4 * D], F32)
    stats = sb1("stats", [128, 40], F32)
    xnb4 = sb1("xnb4", [128, 8, D], BF16)
    xnT = sb1("xnT", [128, 2, PD, 1024], BF16)
    ktile = sb1("ktile", [128, 2, 1024], BF16)
    vtT = sb1("vtT", [128, PD, 1024], BF16)
    vtok = sb1("vtok", [128, 2, D], BF16)

    p1_state = {"xb_free": {}, "xb_free_v": {}, "evac_dma": {}, "vstore": {}}

    PROJ_PS = None

    def ln_part(tb, src_ap):
        par = tb % 2
        if tb >= 2:
            p.wait("sync", sS, p1_state["xb_free"][tb - 2])
            p.wait("sync", sV, p1_state["xb_free_v"][tb - 2])
        t_ld = p.dma("sync", sXb[par], xb[:, par, :, :],
                     src_ap.rearrange("(a p) d -> p a d", p=128))
        p.wait("vector", sXb[par], t_ld)
        p.wait("scalar", sXb[par], t_ld)
        emit_ln_stats(xb[:, par, :, :], 8, stats, sqb)
        tst = sV.n
        p.wait("scalar", sV, tst)
        p.wait("scalar", sT, p1_state.get("xnb_free", 0))
        for i in range(8):
            p.op("scalar", lambda En, par=par, i=i: En.activation(
                out=xnb4[:, i, :], in_=xb[:, par, i, :], func=ACT.Identity,
                scale=stats[:, 16 + i:17 + i], bias=stats[:, 24 + i:25 + i]),
                sem=sS)
        p1_state["apply"] = sS.n
        p1_state["xb_free"][tb] = sS.n
        p1_state["xb_free_v"][tb] = sV.n

    def trans_part(tb):
        par = tb % 2
        p.wait("tensor", sS, sS.n)
        for i in range(8):
            p.wait("tensor", sV, sV.n)
            for c in range(PD):
                ps = psAb if c < 4 else psBb
                p.op("tensor", lambda En, ps=ps, c=c, i=i: En.transpose(
                    out=ps[:, (c % 4) * 128:(c % 4 + 1) * 128],
                    in_=xnb4[:, i, c * 128:(c + 1) * 128],
                    identity=ident_b[:, :]),
                    sem=sT if c % 4 == 3 else None)
            tt4 = sT.n
            p.wait("vector", sT, tt4)
            p.op("vector", lambda En, par=par, i=i: En.tensor_copy(
                out=xnT[:, par, 0:4, i * 128:(i + 1) * 128],
                in_=psAb[:, :].rearrange("p (c t) -> p c t", c=4)), sem=sV)
            p.op("vector", lambda En, par=par, i=i: En.tensor_copy(
                out=xnT[:, par, 4:8, i * 128:(i + 1) * 128],
                in_=psBb[:, :].rearrange("p (c t) -> p c t", c=4)), sem=sV)
        p1_state["xnb_free"] = sT.n
        p1_state["tx"] = sV.n

    def proj_one(tb, wT_sb, dst_h, is_v, dst_tb):
        par = tb % 2
        tx = p1_state["tx"]
        for m in range(PD):
            p.wait("tensor", sV, tx)
            slot = m % 2
            bk = [(psA, psB), (psC, psDq)][slot]
            key = ("proj", slot)
            if key in p1_state:
                p.wait("tensor", sS, p1_state[key])
            for tg in range(2):
                ps = bk[tg]
                for c in range(PD):
                    p.op("tensor",
                         lambda En, ps=ps, wT_sb=wT_sb, c=c, m=m, par=par,
                         tg=tg: En.matmul(
                             out=ps[:, :], lhsT=wT_sb[:, c * PD + m, :],
                             rhs=xnT[:, par, c, tg * 512:(tg + 1) * 512],
                             start=(c == 0), stop=(c == PD - 1)),
                         sem=sT if c == PD - 1 else None)
            tmm = sT.n
            p.wait("scalar", sT, tmm)
            p.wait("scalar", sSt, sSt.n)
            scl = 0.125 if dst_h is qt_h else 1.0
            for tg in range(2):
                ps = bk[tg]
                p.op("scalar", lambda En, ps=ps, m=m, scl=scl, tg=tg:
                     En.activation(
                         out=ktile[:, m % 2, tg * 512:(tg + 1) * 512],
                         in_=ps[:, :], func=ACT.Copy, scale=scl), sem=sS)
            p1_state[("proj", slot)] = sS.n
            if not is_v:
                p.dma("scalar", sSt,
                      dst_h[m, :, dst_tb * 1024:(dst_tb + 1) * 1024],
                      ktile[:, m % 2, :])
            else:
                p.op("scalar", lambda En, m=m: En.copy(
                    out=vtT[:, m, :], in_=ktile[:, m % 2, :]), sem=sS)

    def vtrans_part(tb):
        tvt = sS.n
        p.wait("tensor", sS, tvt)
        for tt in range(8):
            pr = (psAb, psBb) if tt % 2 == 0 else (psCb, psDb)
            key = ("vtp", tt % 2)
            if key in p1_state:
                p.wait("tensor", sS, p1_state[key])
            for m in range(PD):
                ps = pr[0] if m < 4 else pr[1]
                p.op("tensor", lambda En, ps=ps, m=m, tt=tt: En.transpose(
                    out=ps[:, (m % 4) * 128:(m % 4 + 1) * 128],
                    in_=vtT[:, m, tt * 128:(tt + 1) * 128],
                    identity=ident_b[:, :]),
                    sem=sT if m % 4 == 3 else None)
            tt4 = sT.n
            p.wait("scalar", sT, tt4)
            key2 = ("vst", tt % 2)
            if key2 in p1_state:
                p.wait("scalar", sSt2, p1_state[key2])
            p.op("scalar", lambda En, pr=pr, tt=tt: En.copy(
                out=vtok[:, tt % 2, 0:512], in_=pr[0][:, :]), sem=sS)
            p.op("scalar", lambda En, pr=pr, tt=tt: En.copy(
                out=vtok[:, tt % 2, 512:1024], in_=pr[1][:, :]), sem=sS)
            p1_state[("vtp", tt % 2)] = sS.n
            p.wait("gpsimd", sS, sS.n)
            p.dma("gpsimd", sSt2, v_h[tb * 8 + tt, :, :], vtok[:, tt % 2, :])
            p1_state[("vst", tt % 2)] = sSt2.n

    NB1 = S // 1024
    blocks1 = [(tb, x[tb * 1024:(tb + 1) * 1024, :], False, tb)
               for tb in range(NB1)]
    blocks1 += [(NB1 + tb, xq[tb * 1024:(tb + 1) * 1024, :], True, tb)
                for tb in range(2)]
    ln_part(*blocks1[0][:2])
    for bi_, (tb, src, is_q, dtb) in enumerate(blocks1):
        trans_part(tb)
        if is_q:
            proj_one(tb, wqT, qt_h, False, dtb)
            if bi_ + 1 < len(blocks1):
                ln_part(*blocks1[bi_ + 1][:2])
        else:
            proj_one(tb, wkT, kt_h, False, dtb)
            if bi_ + 1 < len(blocks1):
                ln_part(*blocks1[bi_ + 1][:2])
            proj_one(tb, wvT, None, True, dtb)
            vtrans_part(tb)

    if "dbg_kt" in dbg:
        p.wait("gpsimd", sSt, sSt.n)
        p.dma("gpsimd", sSt2, dbg["dbg_kt"][:, :, :], kt_h[:, :, :])
    if "dbg_v" in dbg:
        p.wait("gpsimd", sSt2, sSt2.n)
        p.dma("gpsimd", sSt2, dbg["dbg_v"][:, :, :], v_h[:, :, :])
    if "dbg_qt" in dbg:
        p.wait("gpsimd", sSt, sSt.n)
        p.dma("gpsimd", sSt2, dbg["dbg_qt"][:, :, :], qt_h[:, :, :])

    p.barrier(ALL_SEMS)
    ps1.close()
    st_p1.close()

    st_p2 = contextlib.ExitStack()
    sb2 = lambda name, shape, dt: st_p2.enter_context(nc.sbuf_tensor(name, shape, dt))
    ps2 = contextlib.ExitStack()
    psb2 = lambda name, shape: ps2.enter_context(nc.psum_tensor(name, shape, F32))

    st_moe = contextlib.ExitStack()
    xnT_moe = st_moe.enter_context(nc.sbuf_tensor("xnT_moe", [128, PD, T], BF16))
    cwT = st_moe.enter_context(nc.sbuf_tensor("cwT", [E, T], BF16))

    xm = sb2("xm", [128, 2, 4, D], F32)
    sqb2 = sb2("sqb2", [128, 4 * D], F32)
    stats2 = sb2("stats2", [128, 24], F32)
    xn32 = sb2("xn32", [128, 4, D], F32)
    xnT32 = sb2("xnT32", [128, PD, 512], F32)
    xmT = sb2("xmT", [128, PD, 512], F32)
    centT = sb2("centT", [128, PD, E], F32)
    cent_sb = sb2("cent_sb", [64, D], F32)
    ebias_sb = sb2("ebias_sb", [1, E], F32)
    ebias_t = sb2("ebias_t", [128, E], F32)
    affT = sb2("affT", [64, T], F32)
    aff_tok = sb2("aff_tok", [128, E], F32)
    bia = sb2("bia", [128, E], F32)
    work = sb2("work", [128, E], F32)
    mtile = sb2("mtile", [128, E], F32)
    sig = sb2("sig", [128, E], F32)
    rst = sb2("rst", [128, 4], F32)
    cwb = sb2("cwb", [128, E], BF16)
    psA2 = psb2("psA2", [128, 512])
    psB2 = psb2("psB2", [128, 512])
    psAff = psb2("psAff", [64, 512])
    psB2b = ps2.enter_context(nc.psum_tensor("psB2b", [128, 512], BF16))

    p.dma("sync", sMisc, cent_sb[:, :], cent[:, :])
    t_e = p.dma("sync", sMisc, ebias_sb[:, :], ebias[:, :])
    p.wait("tensor", sMisc, t_e)
    for c in range(PD):
        p.op("tensor", lambda En, c=c: En.transpose(
            out=psA2[:, 0:64], in_=cent_sb[:, c * 128:(c + 1) * 128],
            identity=ident_f[0:64, 0:64]), sem=sT)
        p.wait("vector", sT, sT.n)
        p.op("vector", lambda En, c=c: En.tensor_copy(
            out=centT[:, c, :], in_=psA2[:, 0:64]), sem=sV)
        p.wait("tensor", sV, sV.n)
    p.op("tensor", lambda En: En.matmul(out=psA2[:, 0:64], lhsT=ones_f[:, :],
                                        rhs=ebias_sb[:, :], start=True, stop=True),
         sem=sT)
    p.wait("vector", sT, sT.n)
    p.op("vector", lambda En: En.tensor_copy(out=ebias_t[:, :], in_=psA2[:, 0:64]),
         sem=sV)
    p.wait("tensor", sV, sV.n)

    p2_state = {"xm_free": {}, "xmT_dma": None, "aff_done": 0}
    for bb in range(4):
        par = bb % 2
        if bb >= 2:
            p.wait("sync", sT, p2_state["xm_free"][bb - 2])
            p.wait("sync", sS, p2_state.get("xm_free_s", {}).get(bb - 2, 0))
        t_ld = p.dma("sync", sXb[par], xm[:, par, :, :],
                     xmoe[bb * 512:(bb + 1) * 512, :].rearrange(
                         "(a p) d -> p a d", p=128))
        p.wait("vector", sXb[par], t_ld)
        p.wait("scalar", sXb[par], t_ld)
        emit_ln_stats(xm[:, par, :, :], 4, stats2, sqb2)
        tst = sV.n
        p.wait("scalar", sV, tst)
        p.wait("scalar", sT, sT.n)
        for i in range(4):
            p.op("scalar", lambda En, par=par, i=i: En.activation(
                out=xn32[:, i, :], in_=xm[:, par, i, :], func=ACT.Identity,
                scale=stats2[:, 8 + i:9 + i], bias=stats2[:, 12 + i:13 + i]),
                sem=sS)
        tapply = sS.n
        p2_state.setdefault("xm_free_s", {})[bb] = sS.n
        p.wait("tensor", sS, tapply)
        p.wait("tensor", sT, p2_state["aff_done"])
        for i in range(4):
            p.wait("tensor", sV, sV.n)
            for c in range(PD):
                ps = psA2 if c < 4 else psB2
                p.op("tensor", lambda En, ps=ps, c=c, i=i: En.transpose(
                    out=ps[:, (c % 4) * 128:(c % 4 + 1) * 128],
                    in_=xn32[:, i, c * 128:(c + 1) * 128],
                    identity=ident_f[:, :]),
                    sem=sT if c % 4 == 3 else None)
            tt4 = sT.n
            p.wait("vector", sT, tt4)
            p.op("vector", lambda En, i=i: En.tensor_copy(
                out=xnT32[:, 0:4, i * 128:(i + 1) * 128],
                in_=psA2[:, :].rearrange("p (c t) -> p c t", c=4)), sem=sV)
            p.op("vector", lambda En, i=i: En.tensor_copy(
                out=xnT32[:, 4:8, i * 128:(i + 1) * 128],
                in_=psB2[:, :].rearrange("p (c t) -> p c t", c=4)), sem=sV)
            p.op("vector", lambda En, bb=bb, i=i: En.tensor_copy(
                out=xnT_moe[:, 0:4, (bb * 4 + i) * 128:(bb * 4 + i + 1) * 128],
                in_=psA2[:, :].rearrange("p (c t) -> p c t", c=4)), sem=sV)
            p.op("vector", lambda En, bb=bb, i=i: En.tensor_copy(
                out=xnT_moe[:, 4:8, (bb * 4 + i) * 128:(bb * 4 + i + 1) * 128],
                in_=psB2[:, :].rearrange("p (c t) -> p c t", c=4)), sem=sV)
        if p2_state["xmT_dma"] is not None:
            p.wait("vector", sSt, p2_state["xmT_dma"])
        for i in range(4):
            p.wait("tensor", sV, sV.n)
            for c in range(PD):
                ps = psA2 if c < 4 else psB2
                p.op("tensor", lambda En, ps=ps, c=c, i=i, par=par: En.transpose(
                    out=ps[:, (c % 4) * 128:(c % 4 + 1) * 128],
                    in_=xm[:, par, i, c * 128:(c + 1) * 128],
                    identity=ident_f[:, :]),
                    sem=sT if c % 4 == 3 else None)
            tt4 = sT.n
            p.wait("vector", sT, tt4)
            p.op("vector", lambda En, i=i: En.tensor_copy(
                out=xmT[:, 0:4, i * 128:(i + 1) * 128],
                in_=psA2[:, :].rearrange("p (c t) -> p c t", c=4)), sem=sV)
            p.op("vector", lambda En, i=i: En.tensor_copy(
                out=xmT[:, 4:8, i * 128:(i + 1) * 128],
                in_=psB2[:, :].rearrange("p (c t) -> p c t", c=4)), sem=sV)
        p2_state["xm_free"][bb] = sT.n
        p.wait("scalar", sV, sV.n)
        p.dma("scalar", sSt, xmT_h[:, :, bb * 512:(bb + 1) * 512].rearrange(
            "c p t -> p c t"), xmT[:, :, :])
        p2_state["xmT_dma"] = sSt.n
        p.wait("tensor", sV, sV.n)
        for c in range(PD):
            p.op("tensor", lambda En, c=c: En.matmul(
                out=psAff[:, :], lhsT=centT[:, c, :], rhs=xnT32[:, c, :],
                start=(c == 0), stop=(c == PD - 1)),
                sem=sT if c == PD - 1 else None)
        p2_state["aff_done"] = sT.n
        p.wait("vector", sT, sT.n)
        p.op("vector", lambda En, bb=bb: En.tensor_copy(
            out=affT[:, bb * 512:(bb + 1) * 512], in_=psAff[:, :]), sem=sV)
        p.wait("tensor", sV, sV.n)

    for tt in range(T // 128):
        p.op("tensor", lambda En, tt=tt: En.transpose(
            out=psA2[:, 0:64], in_=affT[:, tt * 128:(tt + 1) * 128],
            identity=ident_f[0:64, 0:64]), sem=sT)
        p.wait("vector", sT, sT.n)
        p.op("vector", lambda En: En.tensor_copy(out=aff_tok[:, :],
                                                 in_=psA2[:, 0:64]), sem=sV)
        p.op("vector", lambda En: En.tensor_add(out=bia[:, :], in0=aff_tok[:, :],
                                                in1=ebias_t[:, :]))
        p.op("vector", lambda En: En.tensor_copy(out=work[:, :], in_=bia[:, :]))
        for it in range(TOPK - 1):
            p.op("vector", lambda En: En.reduce_max(out=rst[:, 0:1], in_=work[:, :],
                                                    axis=AX))
            p.op("vector", lambda En: En.tensor_single_scalar(
                out=mtile[:, :], in_=work[:, :], scalar=rst[:, 0:1], op=ALU.is_ge))
            p.op("vector", lambda En: En.scalar_tensor_tensor(
                out=work[:, :], in0=mtile[:, :], scalar=C_BIG, in1=work[:, :],
                op0=ALU.mult, op1=ALU.add))
        p.op("vector", lambda En: En.reduce_max(out=rst[:, 0:1], in_=work[:, :],
                                                axis=AX))
        p.op("vector", lambda En: En.tensor_single_scalar(
            out=mtile[:, :], in_=bia[:, :], scalar=rst[:, 0:1], op=ALU.is_ge),
            sem=sV)
        p.wait("scalar", sV, sV.n)
        p.op("scalar", lambda En: En.activation(out=sig[:, :], in_=aff_tok[:, :],
                                                func=ACT.Sigmoid), sem=sS)
        p.wait("vector", sS, sS.n)
        p.op("vector", lambda En: En.tensor_mul(out=sig[:, :], in0=sig[:, :],
                                                in1=mtile[:, :]))
        p.op("vector", lambda En: En.reduce_sum(out=rst[:, 1:2], in_=sig[:, :],
                                                axis=AX))
        p.op("vector", lambda En: En.tensor_scalar_add(out=rst[:, 1:2],
                                                       in0=rst[:, 1:2],
                                                       scalar1=C_E8), sem=sV)
        p.wait("vector", sV)
        p.op("vector", lambda En: En.reciprocal(out=rst[:, 2:3], in_=rst[:, 1:2]),
             sem=sV)
        p.wait("vector", sV)
        p.op("vector", lambda En: En.tensor_scalar_mul(out=sig[:, :], in0=sig[:, :],
                                                       scalar1=rst[:, 2:3]))
        p.op("vector", lambda En: En.tensor_copy(out=cwb[:, :], in_=sig[:, :]),
             sem=sV)
        p.wait("tensor", sV, sV.n)
        p.op("tensor", lambda En, tt=tt: En.transpose(
            out=psB2b[0:64, 0:128], in_=cwb[:, :], identity=ident_b[:, :]), sem=sT)
        p.wait("vector", sT, sT.n)
        p.op("vector", lambda En, tt=tt: En.tensor_copy(
            out=cwT[0:64, tt * 128:(tt + 1) * 128], in_=psB2b[0:64, 0:128]),
            sem=sV)
        p.wait("tensor", sV, sV.n)

    if "dbg_cwt" in dbg:
        p.wait("gpsimd", sV, sV.n)
        p.dma("gpsimd", sSt2, dbg["dbg_cwt"][:, :], cwT[:, :])
    if "dbg_aff" in dbg:
        p.wait("gpsimd", sV, sV.n)
        p.dma("gpsimd", sSt2, dbg["dbg_aff"][:, :], affT[:, :])
    if "dbg_xmT" in dbg:
        p.wait("gpsimd", sSt, sSt.n)
        p.dma("gpsimd", sSt2, dbg["dbg_xmT"][:, :, :], xmT_h[:, :, :])

    p.barrier(ALL_SEMS)
    ps2.close()
    st_p2.close()

    st_p3 = contextlib.ExitStack()
    sb3 = lambda name, shape, dt: st_p3.enter_context(nc.sbuf_tensor(name, shape, dt))
    ps3 = contextlib.ExitStack()
    psb3 = lambda name, shape: ps3.enter_context(nc.psum_tensor(name, shape, F32))

    wst = sb3("wst", [128, PD, F], F32)
    wupb = sb3("wupb", [128, 2, PD, F], BF16)
    wdnb = sb3("wdnb", [128, BE, 4, D], BF16)
    gtb = sb3("gtb", [128, BE * 4 * 4, 512], BF16)
    cwrow = sb3("cwrow", [1, BE, 4, 512], BF16)
    ht = sb3("ht", [128, 4, 512], BF16)
    wtb = sb3("wtb", [128, 512], BF16)
    fseg = sb3("fseg", [128, 4, 2, 512], F32)
    psP = [psb3(f"psP{i}", [128, 512]) for i in range(8)]

    GPS = 4

    def up_src(e):
        if e < E:
            return wup[e].rearrange("(c p) f -> p c f", p=128)
        return wsup.rearrange("(c p) f -> p c f", p=128)

    def dn_src(e):
        if e < E:
            return wdn[e].rearrange("(c p) d -> p c d", p=128)
        return wsdn.rearrange("(c p) d -> p c d", p=128)

    up_ready = {}
    dn_ready = {}
    st_free = [0]
    up_slot_user = {0: 0, 1: 0}
    dn_slot_user = {i: 0 for i in range(BE)}
    prev_D_done = [0]

    def load_up(e, slot):
        p.wait("sync", sV, st_free[0])
        t1 = p.dma("sync", sWst, wst[:, :, :], up_src(e))
        p.wait("vector", sWst, t1)
        p.wait("vector", sT, up_slot_user[slot])
        p.op("vector", lambda En, slot=slot: En.tensor_copy(
            out=wupb[:, slot, :, :].rearrange("p c f -> p (c f)"),
            in_=wst[:, :, :].rearrange("p c f -> p (c f)")), sem=sV)
        up_ready[e] = st_free[0] = sV.n

    def load_dn(e, slot):
        p.wait("sync", sV, st_free[0])
        t1 = p.dma("sync", sWst,
                   wst[:, :, :].rearrange("p c f -> p (c f)").rearrange(
                       "p (c d) -> p c d", c=4), dn_src(e))
        p.wait("vector", sWst, t1)
        p.wait("vector", sT, dn_slot_user[slot])
        p.op("vector", lambda En, slot=slot: En.tensor_copy(
            out=wdnb[:, slot, :, :].rearrange("p c d -> p (c d)"),
            in_=wst[:, :, :].rearrange("p c f -> p (c f)")), sem=sV)
        dn_ready[e] = st_free[0] = sV.n

    blocks = [list(range(b * BE, (b + 1) * BE)) for b in range(E // BE)]
    blocks.append([E])

    load_up(blocks[0][0], 0)

    dn_evac_done = {i: 0 for i in range(8)}
    sil_evac_done = {i: 0 for i in range(4)}
    fseg_store = {0: 0, 1: 0, 2: 0, 3: 0}
    ht_consumed = [0]

    first_of_all = True
    cwrow_ready = 0
    for bi, blk in enumerate(blocks):
        nexp = len(blk)
        if blk[0] < E:
            p.wait("sync", sT, cwrow_used if bi > 0 else 0)
            cwrow_ready = p.dma(
                "sync", sCwr,
                cwrow[0:1, 0:nexp, :, :].rearrange("p e g t -> p (e g t)"),
                cwT[blk[0]:blk[0] + nexp, :])
        cwrow_used = 0
        for ei, e in enumerate(blk):
            slot = ei % 2
            p.wait("tensor", sV, up_ready[e])
            for gp in range(GPS):
                for mf in range(4):
                    p.wait("tensor", sS, sil_evac_done[mf])
                    p.wait("tensor", sV, dn_evac_done[mf])
                for mf in range(4):
                    for c in range(PD):
                        p.op("tensor",
                             lambda En, mf=mf, c=c, slot=slot, gp=gp: En.matmul(
                                 out=psP[mf][:, :],
                                 lhsT=wupb[:, slot, c, mf * 128:(mf + 1) * 128],
                                 rhs=xnT_moe[:, c, gp * 512:(gp + 1) * 512],
                                 start=(c == 0), stop=(c == PD - 1)),
                             sem=sT if (c == PD - 1) else None)
                tmm = sT.n
                if e < E:
                    p.wait("tensor", sV, dn_evac_done[7])
                    p.wait("tensor", sCwr, cwrow_ready)
                    p.op("tensor", lambda En, ei=ei, gp=gp: En.matmul(
                        out=psP[7][:, :], lhsT=ones_b[:, :],
                        rhs=cwrow[0:1, ei, gp, :],
                        start=True, stop=True), sem=sT)
                    twt = sT.n
                    cwrow_used = sT.n
                    p.wait("vector", sT, twt)
                    p.op("vector", lambda En: En.tensor_copy(out=wtb[:, :],
                                                             in_=psP[7][:, :]),
                         sem=sV)
                    dn_evac_done[7] = sV.n
                up_slot_user[slot] = tmm
                p.wait("scalar", sT, tmm)
                p.wait("scalar", sV, ht_consumed[0])
                for mf in range(4):
                    p.op("scalar", lambda En, mf=mf: En.activation(
                        out=ht[:, mf, :], in_=psP[mf][:, :], func=ACT.Silu),
                        sem=sS)
                    sil_evac_done[mf] = sS.n
                tsil = sS.n
                gslot = (ei * GPS + gp) * 4 if e < E else gp * 4
                p.wait("vector", sS, tsil)
                p.wait("vector", sT, prev_D_done[0])
                for cf in range(4):
                    if e < E:
                        p.op("vector", lambda En, cf=cf, gslot=gslot: En.tensor_mul(
                            out=gtb[:, gslot + cf, :], in0=ht[:, cf, :],
                            in1=wtb[:, :]), sem=sV)
                    else:
                        p.op("vector", lambda En, cf=cf, gslot=gslot: En.tensor_copy(
                            out=gtb[:, gslot + cf, :], in_=ht[:, cf, :]), sem=sV)
                ht_consumed[0] = sV.n
            nxt = blk[ei + 1] if ei + 1 < nexp else (
                blocks[bi + 1][0] if bi + 1 < len(blocks) else None)
            if nxt is not None:
                load_up(nxt, (ei + 1) % 2)
            load_dn(e, ei)
        gtb_ready = sV.n
        for gp in range(GPS):
            seg_ld = {}
            for ci in range(4):
                if not first_of_all:
                    sl = ci
                    p.wait("gpsimd", sFst[sl], fseg_store[sl])
                    seg_ld[ci] = p.dma(
                        "gpsimd", sFs[sl], fseg[:, sl, :, :],
                        ffn_h[2 * ci:2 * ci + 2, :,
                              gp * 512:(gp + 1) * 512].rearrange("c p t -> p c t"))
            p.wait("tensor", sV, gtb_ready)
            for e in blk:
                p.wait("tensor", sV, dn_ready[e])
            bank_last = {}
            for md in range(8):
                ci = md // 2
                reg = (md % 2) * 256
                p.wait("tensor", sV, dn_evac_done[ci])
                p.wait("tensor", sV, dn_evac_done[4 + ci])
                for ei, e in enumerate(blk):
                    for cf in range(4):
                        gslot = (ei * GPS + gp) * 4 if e < E else gp * 4
                        first0 = (md % 2 == 0 and ei == 0 and cf == 0)
                        last = (md % 2 == 1 and ei == nexp - 1 and cf == 3)
                        p.op("tensor", lambda En, ci=ci, reg=reg, ei=ei, cf=cf,
                             md=md, gslot=gslot, first0=first0, last=last: En.matmul(
                                 out=psP[ci][:, reg:reg + 256],
                                 lhsT=wdnb[:, ei, cf, md * 128:(md + 1) * 128],
                                 rhs=gtb[:, gslot + cf, 0:256],
                                 start=first0, stop=last), sem=None)
                        p.op("tensor", lambda En, ci=ci, reg=reg, ei=ei, cf=cf,
                             md=md, gslot=gslot, first0=first0, last=last:
                             En.matmul(
                                 out=psP[4 + ci][:, reg:reg + 256],
                                 lhsT=wdnb[:, ei, cf, md * 128:(md + 1) * 128],
                                 rhs=gtb[:, gslot + cf, 256:512],
                                 start=first0, stop=last),
                             sem=sT if last else None)
                if md % 2 == 1:
                    bank_last[ci] = sT.n
                    sl = ci
                    p.wait("vector", sT, bank_last[ci])
                    if not first_of_all:
                        p.wait("vector", sFs[ci], seg_ld[ci])
                        for cc in range(2):
                            p.op("vector", lambda En, ci=ci, cc=cc, sl=sl:
                                 En.tensor_add(
                                     out=fseg[:, sl, cc, 0:256],
                                     in0=fseg[:, sl, cc, 0:256],
                                     in1=psP[ci][:, cc * 256:cc * 256 + 256]),
                                 sem=sV)
                            p.op("vector", lambda En, ci=ci, cc=cc, sl=sl:
                                 En.tensor_add(
                                     out=fseg[:, sl, cc, 256:512],
                                     in0=fseg[:, sl, cc, 256:512],
                                     in1=psP[4 + ci][:, cc * 256:cc * 256 + 256]),
                                 sem=sV)
                    else:
                        sl = ci
                        p.wait("vector", sFst[sl], fseg_store[sl])
                        for cc in range(2):
                            p.op("vector", lambda En, ci=ci, cc=cc, sl=sl:
                                 En.tensor_copy(
                                     out=fseg[:, sl, cc, 0:256],
                                     in_=psP[ci][:, cc * 256:cc * 256 + 256]),
                                 sem=sV)
                            p.op("vector", lambda En, ci=ci, cc=cc, sl=sl:
                                 En.tensor_copy(
                                     out=fseg[:, sl, cc, 256:512],
                                     in_=psP[4 + ci][:, cc * 256:cc * 256 + 256]),
                                 sem=sV)
                    dn_evac_done[ci] = sV.n
                    dn_evac_done[4 + ci] = sV.n
                    p.wait("scalar", sV, sV.n)
                    p.dma("scalar", sFst[ci],
                          ffn_h[2 * ci:2 * ci + 2, :,
                                gp * 512:(gp + 1) * 512].rearrange("c p t -> p c t"),
                          fseg[:, ci, :, :])
                    fseg_store[ci] = sFst[ci].n
        prev_D_done[0] = sT.n
        for sl_i in range(len(blk)):
            dn_slot_user[sl_i] = sT.n
        first_of_all = False

    if "dbg_ffn" in dbg:
        p.wait("gpsimd", sSt, sSt.n)
        p.dma("gpsimd", sSt2, dbg["dbg_ffn"][:, :, :], ffn_h[:, :, :])

    p.barrier(ALL_SEMS)
    ps3.close()
    st_p3.close()
    st_moe.close()

    st_p4 = contextlib.ExitStack()
    sb4 = lambda name, shape, dt: st_p4.enter_context(nc.sbuf_tensor(name, shape, dt))
    ps4 = contextlib.ExitStack()
    psb4 = lambda name, shape: ps4.enter_context(nc.psum_tensor(name, shape, F32))

    kbuf = sb4("kbuf", [128, 2, PD, 2048], BF16)
    vbuf = sb4("vbuf", [128, 2, 16, D], BF16)
    qtile = sb4("qtile", [128, 2, PD, 128], BF16)
    acc = sb4("acc", [128, 8, D], F32)
    sc = sb4("sc", [128, 2, 2048], F32)
    pb = sb4("pb", [128, 2, 2048], BF16)
    ptb = sb4("ptb", [128, 16, 128], BF16)
    astat = sb4("astat", [128, NQT, 6], F32)
    obf = sb4("obf", [128, D], BF16)
    psS4 = [psb4(f"psS4{i}", [128, 512]) for i in range(2)]
    psT4 = [ps4.enter_context(nc.psum_tensor(f"psT4{i}", [128, 512], BF16))
            for i in range(4)]
    psPV = [psb4(f"psPV{i}", [128, 512]) for i in range(2)]

    kv_used = {}
    sc_free = {0: 0, 1: 0}
    pb_free = {0: 0, 1: 0}
    st4 = {"pv_done": 0, "prev_tsc": 0, "q_used": {}}

    def scores_part(qg, m, qi, tk):
        q = qg * 8 + qi
        par = m % 2
        qpar = qi % 2
        spar = qi % 2
        key = (qg, m, qi - 2)
        if key in st4["q_used"]:
            p.wait("gpsimd", sT, st4["q_used"][key])
        elif qi < 2 and (qg, m) != (0, 0):
            pkey = ((qg, m - 1, 6 + qi) if m > 0 else (qg - 1, NMC - 1, 6 + qi))
            p.wait("gpsimd", sT, st4["q_used"][pkey])
        tq = p.dma("gpsimd", sQt[qpar], qtile[:, qpar, :, :],
                   qt_h[:, :, q * 128:(q + 1) * 128].rearrange("c p t -> p c t"))
        p.wait("tensor", sKV[par], tk)
        p.wait("tensor", sQt[qpar], tq)
        p.wait("tensor", sS, st4["prev_tsc"])
        for half in range(2):
            if half == 0:
                p.wait("tensor", sS, sc_free[spar])
            else:
                p.wait("tensor", sS, sS.n)
            for c in range(PD):
                for tkb in range(2):
                    p.op("tensor", lambda En, c=c, half=half, tkb=tkb,
                         qpar=qpar, par=par: En.matmul(
                             out=psS4[tkb][:, :],
                             lhsT=qtile[:, qpar, c, :],
                             rhs=kbuf[:, par, c,
                                      (half * 2 + tkb) * 512:
                                      (half * 2 + tkb + 1) * 512],
                             start=(c == 0), stop=(c == PD - 1)),
                         sem=sT if (c == PD - 1 and tkb == 1) else None)
            ts = sT.n
            p.wait("scalar", sT, ts)
            for tkb in range(2):
                p.op("scalar", lambda En, half=half, tkb=tkb, spar=spar:
                     En.copy(out=sc[:, spar, (half * 2 + tkb) * 512:
                                    (half * 2 + tkb + 1) * 512],
                             in_=psS4[tkb][:, :]), sem=sS)
        st4["q_used"][(qg, m, qi)] = sT.n
        st4["prev_tsc"] = sS.n
        return sS.n

    def softpv_part(qg, m, qi, tsc, tv):
        q = qg * 8 + qi
        par = m % 2
        spar = qi % 2
        p.wait("vector", sS, tsc)
        p.op("vector", lambda En, spar=spar, q=q: En.reduce_max(
            out=astat[:, q, 2:3], in_=sc[:, spar, :], axis=AX))
        if m == 0:
            p.op("vector", lambda En, q=q: En.tensor_copy(
                out=astat[:, q, 0:1], in_=astat[:, q, 2:3]))
        else:
            p.op("vector", lambda En, q=q: En.tensor_max(
                out=astat[:, q, 4:5], in0=astat[:, q, 0:1],
                in1=astat[:, q, 2:3]))
            p.op("vector", lambda En, q=q: En.tensor_sub(
                out=astat[:, q, 5:6], in0=astat[:, q, 0:1],
                in1=astat[:, q, 4:5]))
            p.op("vector", lambda En, q=q: En.tensor_copy(
                out=astat[:, q, 0:1], in_=astat[:, q, 4:5]))
        p.op("vector", lambda En, q=q: En.tensor_scalar_mul(
            out=astat[:, q, 3:4], in0=astat[:, q, 0:1], scalar1=C_NEG), sem=sV)
        tmax = sV.n
        p.wait("scalar", sV, tmax)
        if m > 0:
            p.op("scalar", lambda En, q=q: En.activation(
                out=astat[:, q, 2:3], in_=astat[:, q, 5:6], func=ACT.Exp))
        p.wait("scalar", sT, pb_free[spar])
        p.op("scalar", lambda En, spar=spar, q=q: En.activation(
            out=pb[:, spar, :], in_=sc[:, spar, :], func=ACT.Exp,
            bias=astat[:, q, 3:4]), sem=sS)
        texp = sS.n
        sc_free[spar] = texp
        p.wait("vector", sS, texp)
        p.op("vector", lambda En, spar=spar, q=q: En.reduce_sum(
            out=astat[:, q, 4:5], in_=pb[:, spar, :], axis=AX))
        if m == 0:
            p.op("vector", lambda En, q=q: En.tensor_copy(
                out=astat[:, q, 1:2], in_=astat[:, q, 4:5]), sem=sV)
        else:
            p.op("vector", lambda En, q=q: En.scalar_tensor_tensor(
                out=astat[:, q, 1:2], in0=astat[:, q, 1:2],
                scalar=astat[:, q, 2:3], in1=astat[:, q, 4:5],
                op0=ALU.mult, op1=ALU.add), sem=sV)
        p.wait("tensor", sKV[par], tv)
        p.wait("tensor", sS, texp)
        p.wait("tensor", sV, st4["pv_done"])
        for g4 in range(4):
            p.wait("tensor", sS, sS.n)
            for k in range(4):
                kk = g4 * 4 + k
                p.op("tensor", lambda En, g4=g4, k=k, kk=kk, spar=spar:
                     En.transpose(out=psT4[g4][:, k * 128:(k + 1) * 128],
                                  in_=pb[:, spar, kk * 128:(kk + 1) * 128],
                                  identity=ident_b[:, :]),
                     sem=sT if k == 3 else None)
            ttr = sT.n
            p.wait("scalar", sT, ttr)
            p.op("scalar", lambda En, g4=g4: En.copy(
                out=ptb[:, g4 * 4:(g4 + 1) * 4, :].rearrange("p a t -> p (a t)"),
                in_=psT4[g4][:, :]), sem=sS)
            tpt = sS.n
            p.wait("tensor", sS, tpt)
            for k in range(4):
                kk = g4 * 4 + k
                for half in range(2):
                    p.op("tensor", lambda En, kk=kk, half=half, par=par:
                         En.matmul(
                             out=psPV[half][:, :],
                             lhsT=ptb[:, kk, :],
                             rhs=vbuf[:, par, kk, half * 512:(half + 1) * 512],
                             start=(kk == 0), stop=(kk == 15)),
                         sem=sT if (kk == 15 and half == 1) else None)
        pb_free[spar] = sT.n
        tpv = sT.n
        p.wait("vector", sT, tpv)
        for half in range(2):
            if m == 0:
                p.op("vector", lambda En, half=half, qi=qi: En.tensor_copy(
                    out=acc[:, qi, half * 512:(half + 1) * 512],
                    in_=psPV[half][:, :]), sem=sV)
            else:
                p.op("vector", lambda En, half=half, qi=qi, q=q:
                     En.scalar_tensor_tensor(
                         out=acc[:, qi, half * 512:(half + 1) * 512],
                         in0=acc[:, qi, half * 512:(half + 1) * 512],
                         scalar=astat[:, q, 2:3],
                         in1=psPV[half][:, :],
                         op0=ALU.mult, op1=ALU.add), sem=sV)
        st4["pv_done"] = sV.n

    for qg in range(2):
        for m in range(NMC):
            par = m % 2
            if m >= 2 or qg > 0:
                key = (qg, m - 2) if m >= 2 else (qg - 1, NMC - 2 + m)
                p.wait("sync", sT, kv_used[key])
            tk = p.dma("sync", sKV[par], kbuf[:, par, :, :],
                       kt_h[:, :, m * 2048:(m + 1) * 2048].rearrange(
                           "c p t -> p c t"))
            tv = p.dma("sync", sKV[par], vbuf[:, par, :, :],
                       v_h[m * 16:(m + 1) * 16, :, :].rearrange("tt p e -> p tt e"))
            tscs = {}
            tscs[0] = scores_part(qg, m, 0, tk)
            tscs[1] = scores_part(qg, m, 1, tk)
            for qi in range(8):
                softpv_part(qg, m, qi, tscs[qi], tv)
                if qi + 2 < 8:
                    tscs[qi + 2] = scores_part(qg, m, qi + 2, tk)
            kv_used[(qg, m)] = sT.n
        p.wait("vector", sV, sV.n)
        for qi in range(8):
            q = qg * 8 + qi
            p.wait("vector", sT, kv_used[(qg, NMC - 1)])
            p.op("vector", lambda En, q=q: En.reciprocal(
                out=astat[:, q, 5:6], in_=astat[:, q, 1:2]), sem=sV)
            p.wait("vector", sV)
            p.wait("vector", sSt, sSt.n)
            p.op("vector", lambda En, qi=qi, q=q: En.tensor_scalar_mul(
                out=obf[:, :], in0=acc[:, qi, :], scalar1=astat[:, q, 5:6]),
                sem=sV)
            p.wait("scalar", sV, sV.n)
            p.dma("scalar", sSt, out_h[q, :, :], obf[:, :])
        p.wait("sync", sSt, sSt.n)

    if "dbg_out" in dbg:
        p.wait("gpsimd", sSt, sSt.n)
        p.dma("gpsimd", sSt2, dbg["dbg_out"][:, :, :], out_h[:, :, :])

    p.barrier(ALL_SEMS)
    ps4.close()
    st_p4.close()

    st_p5 = contextlib.ExitStack()
    sb5 = lambda name, shape, dt: st_p5.enter_context(nc.sbuf_tensor(name, shape, dt))
    ps5 = contextlib.ExitStack()
    psb5 = lambda name, shape: ps5.enter_context(nc.psum_tensor(name, shape, F32))

    woT = sb5("woT", [128, PD * PD, 128], BF16)
    wld5 = sb5("wld5", [128, 2, D], F32)
    obuf = sb5("obuf", [128, 8, D], BF16)
    resid = sb5("resid", [128, PD, D], F32)
    rload = sb5("rload", [128, PD, D], F32)
    finalT = sb5("finalT", [128, PD, D], F32)
    ybuf = sb5("ybuf", [128, 2, D], F32)
    psA5 = psb5("psA5", [128, 512])
    psB5 = psb5("psB5", [128, 512])
    psC5 = psb5("psC5", [128, 512])
    psD5 = psb5("psD5", [128, 512])

    load_wT(wo, woT, wld5, psA5, psB5)

    cl_done = 0
    for cl in range(2):
        if cl > 0:
            p.wait("sync", sT, cl_done)
            p.wait("gpsimd", sV, cl_done_v)
        tob = p.dma("sync", sMisc, obuf[:, :, :],
                    out_h[cl * 8:(cl + 1) * 8, :, :].rearrange("k p e -> p k e"))
        t1 = p.dma("gpsimd", sRs, resid[:, :, :],
                   xmT_h[:, :, cl * D:(cl + 1) * D].rearrange("c p t -> p c t"))
        t2 = p.dma("gpsimd", sRs, rload[:, :, :],
                   ffn_h[:, :, cl * D:(cl + 1) * D].rearrange("c p t -> p c t"))
        p.wait("vector", sRs, t2)
        for om in range(PD):
            p.op("vector", lambda En, om=om: En.tensor_add(
                out=resid[:, om, :], in0=resid[:, om, :], in1=rload[:, om, :]),
                sem=sV)
        tres = sV.n
        p.wait("tensor", sMisc, tob)
        for om in range(PD):
            p.wait("tensor", sV, sV.n)
            for half in range(2):
                ps = psA5 if half == 0 else psB5
                for k in range(PD):
                    p.op("tensor", lambda En, om=om, half=half, k=k, ps=ps:
                         En.matmul(out=ps[:, :], lhsT=woT[:, k * PD + om, :],
                                   rhs=obuf[:, k, half * 512:(half + 1) * 512],
                                   start=(k == 0), stop=(k == PD - 1)),
                         sem=sT if k == PD - 1 else None)
            tmm = sT.n
            p.wait("vector", sT, tmm)
            p.wait("vector", sV, tres)
            for half in range(2):
                ps = psA5 if half == 0 else psB5
                p.op("vector", lambda En, om=om, half=half, ps=ps: En.tensor_add(
                    out=finalT[:, om, half * 512:(half + 1) * 512],
                    in0=resid[:, om, half * 512:(half + 1) * 512],
                    in1=ps[:, :]), sem=sV)
        tfin = sV.n
        cl_done_v = sV.n
        p.wait("tensor", sV, tfin)
        ystore = {0: 0, 1: 0}
        for et in range(PD):
            par = et % 2
            p.wait("vector", sY[par], ystore[par])
            for om in range(PD):
                ps = psC5 if om % 2 == 0 else psD5
                p.wait("tensor", sV, sV.n)
                p.op("tensor", lambda En, et=et, om=om, ps=ps: En.transpose(
                    out=ps[:, 0:128], in_=finalT[:, om, et * 128:(et + 1) * 128],
                    identity=ident_f[:, :]), sem=sT)
                p.wait("vector", sT, sT.n)
                p.op("vector", lambda En, om=om, ps=ps, par=par: En.tensor_copy(
                    out=ybuf[:, par, om * 128:(om + 1) * 128], in_=ps[:, 0:128]),
                    sem=sV)
            p.wait("scalar", sV, sV.n)
            p.dma("scalar", sY[par],
                  y[cl * D + et * 128: cl * D + (et + 1) * 128, :],
                  ybuf[:, par, :])
            ystore[par] = sY[par].n
        cl_done = sT.n

    p.wait("scalar", sY[0], sY[0].n)
    p.wait("scalar", sY[1], sY[1].n)
    p.wait("sync", sY[0], sY[0].n)
    p.wait("sync", sY[1], sY[1].n)
    ps5.close()
    st_p5.close()
    st_glob.close()

    p.finalize()
    return nc


_NC_CACHE = {}


def _get_nc(debug=()):
    key = tuple(sorted(debug))
    if key not in _NC_CACHE:
        _NC_CACHE[key] = build_nc(debug=key)
    return _NC_CACHE[key]


def make_in_maps(inputs):
    x = np.ascontiguousarray(np.asarray(inputs["L0_attn_q"],
                                        dtype=np.float32).reshape(S, D))
    base = {
        "x": x,
        "wq": np.ascontiguousarray(np.asarray(inputs["L0_attn_wq"], np.float32)),
        "wk": np.ascontiguousarray(np.asarray(inputs["L0_attn_wk"], np.float32)),
        "wv": np.ascontiguousarray(np.asarray(inputs["L0_attn_wv"], np.float32)),
        "wo": np.ascontiguousarray(np.asarray(inputs["L0_attn_wo"], np.float32)),
        "cent": np.ascontiguousarray(np.asarray(inputs["L0_ffn_centroids"],
                                                np.float32)),
        "ebias": np.ascontiguousarray(np.asarray(inputs["L0_ffn_expert_biases"],
                                                 np.float32).reshape(1, E)),
        "wsup": np.ascontiguousarray(np.asarray(inputs["L0_ffn_Ws_up"], np.float32)),
        "wsdn": np.ascontiguousarray(np.asarray(inputs["L0_ffn_Ws_down"],
                                                np.float32)),
        "wup": np.ascontiguousarray(np.asarray(inputs["L0_ffn_W_up"], np.float32)),
        "wdn": np.ascontiguousarray(np.asarray(inputs["L0_ffn_W_down"], np.float32)),
        "ident_b_h": np.eye(128, dtype=ml_dtypes.bfloat16),
        "ident_f_h": np.eye(128, dtype=np.float32),
        "ones_b_h": np.ones((1, 128), dtype=ml_dtypes.bfloat16),
        "ones_f_h": np.ones((1, 128), dtype=np.float32),
        "cpool_h": np.tile(np.array(
            [[EPS, 0.125, 1.0, 1e-8, 1.0 / D, -1.0, -1e30, 0.0]], np.float32),
            (128, 1)),
    }
    e_idx = np.arange(D)
    in_maps = []
    for j in range(NCORES):
        m = dict(base)
        m["xq"] = np.ascontiguousarray(x[j * T:(j + 1) * T])
        rows = np.concatenate([e_idx * 16 + (2 * j), e_idx * 16 + (2 * j + 1)])
        m["xmoe"] = np.ascontiguousarray(x[rows])
        in_maps.append(m)
    return in_maps


def assemble(results):
    out = np.empty((S, D), dtype=np.float32)
    e_idx = np.arange(D)
    for j in range(NCORES):
        yj = np.asarray(results[j]["y"]).reshape(2, D, D)
        out[e_idx * 16 + 2 * j] = yj[0]
        out[e_idx * 16 + 2 * j + 1] = yj[1]
    return out.reshape(1, S, D)


def kernel(**inputs):
    nc = _get_nc()
    in_maps = make_in_maps(inputs)
    res = run_bass_kernel_spmd(nc, in_maps, core_ids=list(range(NCORES)))
    return assemble(res.results)


# version 10
# speedup vs baseline: 1.4100x; 1.4100x over previous
import os
os.environ.setdefault("BASS_INTERP_NUM_WORKERS", "8")

import contextlib
import numpy as np
import ml_dtypes

import concourse.bass as bass
import concourse.mybir as mybir
from concourse.bacc import Bacc
from concourse.bass_utils import run_bass_kernel_spmd

F32 = mybir.dt.float32
BF16 = mybir.dt.bfloat16
I16 = mybir.dt.int16
ACT = mybir.ActivationFunctionType
ALU = mybir.AluOpType
AX = mybir.AxisListType.X

NCORES = 8
S = 16384
D = 1024
T = S // NCORES
PD = D // 128
F = 512
E = 64
TOPK = 8
EPS = 1e-5
NTB = S // 512
SERIAL = bool(int(os.environ.get("KERNEL_SERIAL", "0")))
NQT = T // 128
NMC = S // 2048
CAP = 768
EH = E // 2
NHALF = EH * CAP + 128
ZROW = EH * CAP

ENGINES = ("sync", "gpsimd", "vector", "scalar", "tensor")


class Sem:
    __slots__ = ("name", "n", "h")

    def __init__(self, name):
        self.name = name
        self.n = 0
        self.h = None


class Prog:
    def __init__(self, nc):
        self.nc = nc
        self.ops = {e: [] for e in ENGINES}
        self.sems = []

    def sem(self, name):
        s = Sem(name)
        self.sems.append(s)
        return s

    def wait(self, eng, sem, tgt=None):
        t = sem.n if tgt is None else tgt
        if t > 0:
            self.ops[eng].append(lambda En, s=sem, t=t: En.wait_ge(s.h, t))

    def dma(self, eng, sem, out, in_):
        aser = self.auto_serial.get(eng)
        if aser is not None:
            self.wait(eng, aser)
        sem.n += 16
        self.ops[eng].append(
            lambda En, o=out, i=in_, s=sem: En.dma_start(out=o, in_=i).then_inc(s.h, 16))
        return sem.n

    def opdma(self, eng, sem, f):
        aser = self.auto_serial.get(eng)
        if aser is not None:
            self.wait(eng, aser)
        sem.n += 16
        self.ops[eng].append(lambda En, f=f, s=sem: f(En).then_inc(s.h, 16))
        return sem.n

    auto_serial = {}

    def op(self, eng, f, sem=None):
        aser = self.auto_serial.get(eng)
        if aser is not None:
            if sem is None:
                sem = aser
            self.wait(eng, aser)
        if sem is not None:
            sem.n += 1

            def g(En, f=f, s=sem):
                f(En).then_inc(s.h, 1)
        else:
            def g(En, f=f):
                f(En)
        self.ops[eng].append(g)
        return sem.n if sem is not None else None

    def barrier(self, sems):
        for eng in ENGINES:
            for s in sems:
                self.wait(eng, s)

    def finalize(self):
        nc = self.nc
        with contextlib.ExitStack() as es:
            for s in self.sems:
                s.h = es.enter_context(nc.semaphore(s.name))
            with nc.Block() as block:
                block.sync(lambda En: [f(En) for f in self.ops["sync"]])
                block.gpsimd(lambda En: [f(En) for f in self.ops["gpsimd"]])
                block.vector(lambda En: [f(En) for f in self.ops["vector"]])
                block.scalar(lambda En: [f(En) for f in self.ops["scalar"]])
                block.tensor(lambda En: [f(En) for f in self.ops["tensor"]])


def build_nc(debug=()):
    nc = Bacc("TRN2", target_bir_lowering=False,
              detect_race_conditions=False)
    p = Prog(nc)

    x = nc.dram_tensor("x", [S, D], F32, kind="ExternalInput").ap()
    xq = nc.dram_tensor("xq", [T, D], F32, kind="ExternalInput").ap()
    xmoe = nc.dram_tensor("xmoe", [T, D], F32, kind="ExternalInput").ap()
    wq = nc.dram_tensor("wq", [D, D], F32, kind="ExternalInput").ap()
    wk = nc.dram_tensor("wk", [D, D], F32, kind="ExternalInput").ap()
    wv = nc.dram_tensor("wv", [D, D], F32, kind="ExternalInput").ap()
    wo = nc.dram_tensor("wo", [D, D], F32, kind="ExternalInput").ap()
    cent = nc.dram_tensor("cent", [E, D], F32, kind="ExternalInput").ap()
    ebias = nc.dram_tensor("ebias", [1, E], F32, kind="ExternalInput").ap()
    wsup = nc.dram_tensor("wsup", [D, F], F32, kind="ExternalInput").ap()
    wsdn = nc.dram_tensor("wsdn", [F, D], F32, kind="ExternalInput").ap()
    wup = nc.dram_tensor("wup", [E, D, F], F32, kind="ExternalInput").ap()
    wdn = nc.dram_tensor("wdn", [E, F, D], F32, kind="ExternalInput").ap()
    y = nc.dram_tensor("y", [T, D], F32, kind="ExternalOutput").ap()
    ident_b_h = nc.dram_tensor("ident_b_h", [128, 128], BF16,
                               kind="ExternalInput").ap()
    ident_f_h = nc.dram_tensor("ident_f_h", [128, 128], F32,
                               kind="ExternalInput").ap()
    ones_b_h = nc.dram_tensor("ones_b_h", [1, 128], BF16,
                              kind="ExternalInput").ap()
    ones_f_h = nc.dram_tensor("ones_f_h", [1, 128], F32,
                              kind="ExternalInput").ap()
    cpool_h = nc.dram_tensor("cpool_h", [128, 16], F32, kind="ExternalInput").ap()
    ecb_h = nc.dram_tensor("ecb_h", [128, E], F32, kind="ExternalInput").ap()

    kt_h = nc.dram_tensor("kt_h", [PD, 128, S], BF16, kind="Internal").ap()
    v_h = nc.dram_tensor("v_h", [S // 128, 128, D], BF16, kind="Internal").ap()
    qt_h = nc.dram_tensor("qt_h", [PD, 128, T], BF16, kind="Internal").ap()
    out_h = nc.dram_tensor("out_h", [NQT, 128, D], BF16, kind="Internal").ap()
    xnm_h = nc.dram_tensor("xnm_h", [T, D], BF16, kind="Internal").ap()
    tok_h = nc.dram_tensor("tok_h", [E, CAP], I16, kind="Internal").ap()
    sxtok_h = nc.dram_tensor("sxtok_h", [T, 16], I16, kind="Internal").ap()
    yall_h = nc.dram_tensor("yall_h", [2 * NHALF, D], BF16, kind="Internal").ap()
    ysh_h = nc.dram_tensor("ysh_h", [T, D], F32, kind="Internal").ap()

    dbg = {}
    for name in debug:
        shp = {"dbg_kt": [PD, 128, S], "dbg_v": [S // 128, 128, D],
               "dbg_qt": [PD, 128, T], "dbg_out": [NQT, 128, D],
               "dbg_cwt": [E, T], "dbg_aff": [E, T],
               "dbg_tok": [E, CAP], "dbg_gate": [E, CAP],
               "dbg_rank": [128, NQT, E], "dbg_sidx": [128, NQT, 16],
               "dbg_yall": [2 * NHALF, D], "dbg_ysh": [T, D],
               "dbg_facc": [128, NQT, D]}[name]
        dt = (BF16 if name in ("dbg_kt", "dbg_v", "dbg_qt", "dbg_out",
                               "dbg_gate", "dbg_yall") else
              I16 if name in ("dbg_tok",) else F32)
        dbg[name] = nc.dram_tensor(name, shp, dt, kind="ExternalOutput").ap()

    sLd = p.sem("sLd")
    sMisc = p.sem("sMisc")
    sWld = [p.sem("sWld0"), p.sem("sWld1")]
    sXb = [p.sem("sXb0"), p.sem("sXb1")]
    sWst = p.sem("sWst")
    sCwr = p.sem("sCwr")
    sFs = [p.sem(f"sFs{i}") for i in range(4)]
    sFst = [p.sem(f"sFst{i}") for i in range(4)]
    sKV = [p.sem("sKV0"), p.sem("sKV1")]
    sQt = [p.sem("sQt0"), p.sem("sQt1")]
    sY = [p.sem("sY0"), p.sem("sY1")]
    sRs = p.sem("sRs")
    sSt = p.sem("sSt")
    sSt2 = p.sem("sSt2")
    sT = p.sem("sT")
    sV = p.sem("sV")
    sS = p.sem("sS")
    sG = p.sem("sG")
    sGa = p.sem("sGa")
    sYs = p.sem("sYs")
    sIx = p.sem("sIx")
    ALL_SEMS = ([sLd, sMisc, sWst, sCwr, sSt, sSt2, sT, sV, sS, sG,
                 sGa, sYs, sIx]
                + sWld + sXb + sFs + sFst + sKV + sQt + sY + [sRs])
    p.auto_serial = {"vector": sV, "scalar": sS}

    st_glob = contextlib.ExitStack()
    sbg = lambda name, shape, dt: st_glob.enter_context(nc.sbuf_tensor(name, shape, dt))

    ident_b = sbg("ident_b", [128, 128], BF16)
    ident_f = sbg("ident_f", [128, 128], F32)
    ones_b = sbg("ones_b", [1, 128], BF16)
    ones_f = sbg("ones_f", [1, 128], F32)

    cpool = sbg("cpool", [128, 16], F32)
    p.dma("sync", sMisc, ident_b[:, :], ident_b_h[:, :])
    p.dma("sync", sMisc, ident_f[:, :], ident_f_h[:, :])
    p.dma("sync", sMisc, ones_b[:, :], ones_b_h[:, :])
    p.dma("sync", sMisc, ones_f[:, :], ones_f_h[:, :])
    p.dma("sync", sMisc, cpool[:, :], cpool_h[:, :])
    CONSTS = [EPS, 0.125, 1.0, 1e-8, 1.0 / D, -1.0, -1e30, 0.0,
              float(ZROW), float(-ZROW), 2.0, 3.0, 4.0, 5.0, 6.0, 7.0]
    for ci_, cv_ in enumerate(CONSTS):
        nc.const_aps.register(F32, cv_, cpool[:, ci_:ci_ + 1])
    (C_EPS, C_QS, C_ONE, C_E8, C_RD, C_NEG, C_BIG, C_ZERO,
     C_Z, C_NZ, C_I2, C_I3, C_I4, C_I5, C_I6, C_I7) = (
        cpool[:, i:i + 1] for i in range(16))
    C_IT = [None, None, C_I2, C_I3, C_I4, C_I5, C_I6, C_I7]
    CONSTS_LD = sMisc.n
    p.wait("tensor", sMisc, CONSTS_LD)
    p.wait("scalar", sMisc, CONSTS_LD)
    p.wait("vector", sMisc, CONSTS_LD)
    p.wait("gpsimd", sMisc, CONSTS_LD)

    wld_carry = {0: 0, 1: 0}

    def load_wT(w_ap, wT_sb, wld, psA, psB):
        tr_done = {}
        for m in range(PD):
            p.wait("sync", sT, wld_carry[m % 2])
            t_ld = p.dma("sync", sWld[m % 2], wld[:, m % 2, :],
                         w_ap[m * 128:(m + 1) * 128, :])
            p.wait("tensor", sWld[m % 2], t_ld)
            ps = psA if (m % 2 == 0) else psB
            for c in range(PD):
                if c % 4 == 0:
                    p.wait("tensor", sV, sV.n)
                p.op("tensor", lambda En, ps=ps, c=c, m=m: En.transpose(
                    out=ps[:, (c % 4) * 128:(c % 4 + 1) * 128],
                    in_=wld[:, m % 2, c * 128:(c + 1) * 128],
                    identity=ident_f[:, :]),
                    sem=sT if c % 4 == 3 else None)
                if c % 4 == 3:
                    tt = sT.n
                    p.wait("vector", sT, tt)
                    for cc in range(c - 3, c + 1):
                        p.op("vector", lambda En, ps=ps, cc=cc, m=m: En.tensor_copy(
                            out=wT_sb[:, cc * PD + m, :],
                            in_=ps[:, (cc % 4) * 128:(cc % 4 + 1) * 128]),
                            sem=sV)
            tr_done[m] = wld_carry[m % 2] = sT.n

    def emit_ln_stats(xin_g, G, stats, sqb, hstep=4):
        stats = stats[:, :]
        v, s = "vector", "scalar"
        p.op(v, lambda En: En.reduce_sum(out=stats[:, 0:G], in_=xin_g, axis=AX))
        p.op(v, lambda En: En.tensor_scalar_mul(out=stats[:, 0:G],
                                                in0=stats[:, 0:G], scalar1=C_RD))
        for h in range(0, G, hstep):
            w = min(hstep, G - h)
            sqv = sqb[:, 0:w * D].rearrange("p (g d) -> p g d", g=w)
            if h > 0:
                p.wait(s, sV, sV.n)
            p.op(s, lambda En, h=h, w=w: En.activation(
                out=sqb[:, 0:w * D],
                in_=xin_g[:, h:h + w, :].rearrange("p g d -> p (g d)"),
                func=ACT.Square), sem=sS)
            p.wait(v, sS)
            p.op(v, lambda En, h=h, w=w, sqv=sqv: En.reduce_sum(
                out=stats[:, G + h:G + h + w], in_=sqv, axis=AX), sem=sV)
        p.op(v, lambda En: En.tensor_scalar_mul(out=stats[:, G:2 * G],
                                                in0=stats[:, G:2 * G], scalar1=C_RD))
        p.op(v, lambda En: En.tensor_mul(out=stats[:, 4 * G:5 * G],
                                         in0=stats[:, 0:G], in1=stats[:, 0:G]))
        p.op(v, lambda En: En.tensor_sub(out=stats[:, G:2 * G],
                                         in0=stats[:, G:2 * G],
                                         in1=stats[:, 4 * G:5 * G]), sem=sV)
        p.wait(s, sV)
        p.op(s, lambda En: En.activation(out=stats[:, G:2 * G],
                                         in_=stats[:, G:2 * G],
                                         func=ACT.Sqrt, bias=EPS), sem=sS)
        p.wait(v, sS)
        p.op(v, lambda En: En.reciprocal(out=stats[:, 2 * G:3 * G],
                                         in_=stats[:, G:2 * G]), sem=sV)
        p.wait(v, sV)
        p.op(v, lambda En: En.tensor_mul(out=stats[:, 3 * G:4 * G],
                                         in0=stats[:, 0:G],
                                         in1=stats[:, 2 * G:3 * G]))
        p.op(v, lambda En: En.tensor_scalar_mul(out=stats[:, 3 * G:4 * G],
                                                in0=stats[:, 3 * G:4 * G],
                                                scalar1=C_NEG), sem=sV)

    st_p1 = contextlib.ExitStack()
    sb1 = lambda name, shape, dt: st_p1.enter_context(nc.sbuf_tensor(name, shape, dt))
    ps1 = contextlib.ExitStack()
    psb1 = lambda name, shape: ps1.enter_context(nc.psum_tensor(name, shape, F32))

    wqT = sb1("wqT", [128, PD * PD, 128], BF16)
    wkT = sb1("wkT", [128, PD * PD, 128], BF16)
    wvT = sb1("wvT", [128, PD * PD, 128], BF16)
    psA = psb1("psA", [128, 512])
    psB = psb1("psB", [128, 512])
    psC = psb1("psC", [128, 512])
    psDq = psb1("psDq", [128, 512])
    psAb = ps1.enter_context(nc.psum_tensor("psAb", [128, 512], BF16))
    psBb = ps1.enter_context(nc.psum_tensor("psBb", [128, 512], BF16))
    psCb = ps1.enter_context(nc.psum_tensor("psCb", [128, 512], BF16))
    psDb = ps1.enter_context(nc.psum_tensor("psDb", [128, 512], BF16))

    st_wld = contextlib.ExitStack()
    wld = st_wld.enter_context(nc.sbuf_tensor("wld", [128, 2, D], F32))
    load_wT(wq, wqT, wld, psA, psB)
    load_wT(wk, wkT, wld, psA, psB)
    load_wT(wv, wvT, wld, psA, psB)
    st_wld.close()
    p.wait("sync", sT, sT.n)

    xb = sb1("xb", [128, 2, 8, D], F32)
    sqb = sb1("sqb", [128, 4 * D], F32)
    stats = sb1("stats", [128, 40], F32)
    xnb4 = sb1("xnb4", [128, 8, D], BF16)
    xnT = sb1("xnT", [128, 2, PD, 1024], BF16)
    ktile = sb1("ktile", [128, 2, 1024], BF16)
    vtT = sb1("vtT", [128, PD, 1024], BF16)
    vtok = sb1("vtok", [128, 2, D], BF16)

    p1_state = {"xb_free": {}, "xb_free_v": {}, "evac_dma": {}, "vstore": {}}

    PROJ_PS = None

    def ln_part(tb, src_ap):
        par = tb % 2
        if tb >= 2:
            p.wait("sync", sS, p1_state["xb_free"][tb - 2])
            p.wait("sync", sV, p1_state["xb_free_v"][tb - 2])
        t_ld = p.dma("sync", sXb[par], xb[:, par, :, :],
                     src_ap.rearrange("(a p) d -> p a d", p=128))
        p.wait("vector", sXb[par], t_ld)
        p.wait("scalar", sXb[par], t_ld)
        emit_ln_stats(xb[:, par, :, :], 8, stats, sqb)
        tst = sV.n
        p.wait("scalar", sV, tst)
        p.wait("scalar", sT, p1_state.get("xnb_free", 0))
        for i in range(8):
            p.op("scalar", lambda En, par=par, i=i: En.activation(
                out=xnb4[:, i, :], in_=xb[:, par, i, :], func=ACT.Identity,
                scale=stats[:, 16 + i:17 + i], bias=stats[:, 24 + i:25 + i]),
                sem=sS)
        p1_state["apply"] = sS.n
        p1_state["xb_free"][tb] = sS.n
        p1_state["xb_free_v"][tb] = sV.n

    def trans_part(tb):
        par = tb % 2
        p.wait("tensor", sS, sS.n)
        for i in range(8):
            p.wait("tensor", sV, sV.n)
            for c in range(PD):
                ps = psAb if c < 4 else psBb
                p.op("tensor", lambda En, ps=ps, c=c, i=i: En.transpose(
                    out=ps[:, (c % 4) * 128:(c % 4 + 1) * 128],
                    in_=xnb4[:, i, c * 128:(c + 1) * 128],
                    identity=ident_b[:, :]),
                    sem=sT if c % 4 == 3 else None)
            tt4 = sT.n
            p.wait("vector", sT, tt4)
            p.op("vector", lambda En, par=par, i=i: En.tensor_copy(
                out=xnT[:, par, 0:4, i * 128:(i + 1) * 128],
                in_=psAb[:, :].rearrange("p (c t) -> p c t", c=4)), sem=sV)
            p.op("vector", lambda En, par=par, i=i: En.tensor_copy(
                out=xnT[:, par, 4:8, i * 128:(i + 1) * 128],
                in_=psBb[:, :].rearrange("p (c t) -> p c t", c=4)), sem=sV)
        p1_state["xnb_free"] = sT.n
        p1_state["tx"] = sV.n

    def proj_one(tb, wT_sb, dst_h, is_v, dst_tb):
        par = tb % 2
        tx = p1_state["tx"]
        for m in range(PD):
            p.wait("tensor", sV, tx)
            slot = m % 2
            bk = [(psA, psB), (psC, psDq)][slot]
            key = ("proj", slot)
            if key in p1_state:
                p.wait("tensor", sS, p1_state[key])
            for tg in range(2):
                ps = bk[tg]
                for c in range(PD):
                    p.op("tensor",
                         lambda En, ps=ps, wT_sb=wT_sb, c=c, m=m, par=par,
                         tg=tg: En.matmul(
                             out=ps[:, :], lhsT=wT_sb[:, c * PD + m, :],
                             rhs=xnT[:, par, c, tg * 512:(tg + 1) * 512],
                             start=(c == 0), stop=(c == PD - 1)),
                         sem=sT if c == PD - 1 else None)
            tmm = sT.n
            p.wait("scalar", sT, tmm)
            p.wait("scalar", sSt, sSt.n)
            scl = 0.125 if dst_h is qt_h else 1.0
            for tg in range(2):
                ps = bk[tg]
                p.op("scalar", lambda En, ps=ps, m=m, scl=scl, tg=tg:
                     En.activation(
                         out=ktile[:, m % 2, tg * 512:(tg + 1) * 512],
                         in_=ps[:, :], func=ACT.Copy, scale=scl), sem=sS)
            p1_state[("proj", slot)] = sS.n
            if not is_v:
                p.dma("scalar", sSt,
                      dst_h[m, :, dst_tb * 1024:(dst_tb + 1) * 1024],
                      ktile[:, m % 2, :])
            else:
                p.op("scalar", lambda En, m=m: En.copy(
                    out=vtT[:, m, :], in_=ktile[:, m % 2, :]), sem=sS)

    def vtrans_part(tb):
        tvt = sS.n
        p.wait("tensor", sS, tvt)
        for tt in range(8):
            pr = (psAb, psBb) if tt % 2 == 0 else (psCb, psDb)
            key = ("vtp", tt % 2)
            if key in p1_state:
                p.wait("tensor", sS, p1_state[key])
            for m in range(PD):
                ps = pr[0] if m < 4 else pr[1]
                p.op("tensor", lambda En, ps=ps, m=m, tt=tt: En.transpose(
                    out=ps[:, (m % 4) * 128:(m % 4 + 1) * 128],
                    in_=vtT[:, m, tt * 128:(tt + 1) * 128],
                    identity=ident_b[:, :]),
                    sem=sT if m % 4 == 3 else None)
            tt4 = sT.n
            p.wait("scalar", sT, tt4)
            key2 = ("vst", tt % 2)
            if key2 in p1_state:
                p.wait("scalar", sSt2, p1_state[key2])
            p.op("scalar", lambda En, pr=pr, tt=tt: En.copy(
                out=vtok[:, tt % 2, 0:512], in_=pr[0][:, :]), sem=sS)
            p.op("scalar", lambda En, pr=pr, tt=tt: En.copy(
                out=vtok[:, tt % 2, 512:1024], in_=pr[1][:, :]), sem=sS)
            p1_state[("vtp", tt % 2)] = sS.n
            p.wait("gpsimd", sS, sS.n)
            p.dma("gpsimd", sSt2, v_h[tb * 8 + tt, :, :], vtok[:, tt % 2, :])
            p1_state[("vst", tt % 2)] = sSt2.n

    NB1 = S // 1024
    blocks1 = [(tb, x[tb * 1024:(tb + 1) * 1024, :], False, tb)
               for tb in range(NB1)]
    blocks1 += [(NB1 + tb, xq[tb * 1024:(tb + 1) * 1024, :], True, tb)
                for tb in range(2)]
    ln_part(*blocks1[0][:2])
    for bi_, (tb, src, is_q, dtb) in enumerate(blocks1):
        trans_part(tb)
        if is_q:
            proj_one(tb, wqT, qt_h, False, dtb)
            if bi_ + 1 < len(blocks1):
                ln_part(*blocks1[bi_ + 1][:2])
        else:
            proj_one(tb, wkT, kt_h, False, dtb)
            if bi_ + 1 < len(blocks1):
                ln_part(*blocks1[bi_ + 1][:2])
            proj_one(tb, wvT, None, True, dtb)
            vtrans_part(tb)

    if "dbg_kt" in dbg:
        p.wait("gpsimd", sSt, sSt.n)
        p.dma("gpsimd", sSt2, dbg["dbg_kt"][:, :, :], kt_h[:, :, :])
    if "dbg_v" in dbg:
        p.wait("gpsimd", sSt2, sSt2.n)
        p.dma("gpsimd", sSt2, dbg["dbg_v"][:, :, :], v_h[:, :, :])
    if "dbg_qt" in dbg:
        p.wait("gpsimd", sSt, sSt.n)
        p.dma("gpsimd", sSt2, dbg["dbg_qt"][:, :, :], qt_h[:, :, :])

    p.barrier(ALL_SEMS)
    ps1.close()
    st_p1.close()

    st_p2 = contextlib.ExitStack()
    sb2 = lambda name, shape, dt: st_p2.enter_context(nc.sbuf_tensor(name, shape, dt))
    ps2 = contextlib.ExitStack()
    psb2 = lambda name, shape: ps2.enter_context(nc.psum_tensor(name, shape, F32))

    st_tab5 = contextlib.ExitStack()
    idxrank = st_tab5.enter_context(
        nc.sbuf_tensor("idxrank", [128, 8, 2, T // 16], I16))
    st_tab = contextlib.ExitStack()
    tok_l = st_tab.enter_context(nc.sbuf_tensor("tok_l", [E, CAP], I16))
    gatesc = st_tab.enter_context(nc.sbuf_tensor("gatesc", [128, CAP // 128, E], F32))
    idx_all = st_tab.enter_context(nc.sbuf_tensor("idx_all", [128, E, CAP // 16], I16))

    st_moe = contextlib.ExitStack()
    xnT_moe = st_moe.enter_context(nc.sbuf_tensor("xnT_moe", [128, PD, T], BF16))
    cwT = st_moe.enter_context(nc.sbuf_tensor("cwT", [E, T], BF16))

    xm = sb2("xm", [128, 2, 4, D], F32)
    sqb2 = sb2("sqb2", [128, 2 * D], F32)
    stats2 = sb2("stats2", [128, 24], F32)
    xn32 = sb2("xn32", [128, 4, D], F32)
    xnmb = sb2("xnmb", [128, 4, D], BF16)
    xnT32 = sb2("xnT32", [128, PD, 512], F32)
    centT = sb2("centT", [128, PD, E], F32)
    cent_sb = sb2("cent_sb", [64, D], F32)
    ebias_sb = sb2("ebias_sb", [1, E], F32)
    ebias_t = sb2("ebias_t", [128, E], F32)
    affT = sb2("affT", [64, T], F32)
    aff_tok = sb2("aff_tok", [128, E], F32)
    bia = sb2("bia", [128, E], F32)
    work = sb2("work", [128, E], F32)
    mtile = sb2("mtile", [128, E], F32)
    sig = sb2("sig", [128, E], F32)
    rst = sb2("rst", [128, 4], F32)
    cwb = sb2("cwb", [128, E], BF16)
    maskT = sb2("maskT", [E, T], F32)
    onesET = sb2("onesET", [E, T], F32)
    sposc = sb2("sposc", [E, T], F32)
    sidxf = onesET
    sidx16 = sb2("sidx16", [E, T], I16)
    iota16 = sb2("iota16", [E, T], I16)
    wrmap16 = sb2("wrmap16", [E, CAP], I16)
    tok_lw = sb2("tok_lw", [E, CAP], I16)
    idxrank5 = sb2("idxrank5", [16, T // 16, 8, 2], I16)
    gates_l = sb2("gates_l", [E, CAP], BF16)
    gates32 = sb2("gates32", [E, CAP], F32)
    rankT = sb2("rankT", [128, NQT, E], F32)
    rnk = sb2("rnk", [128, E], F32)
    macc = sb2("macc", [128, E], F32)
    ecb = sb2("ecb", [128, E], F32)
    spT2 = sb2("spT2", [128, E], F32)
    mrank = sb2("mrank", [128, E], F32)
    mt0 = sb2("mt0", [128, E], F32)
    red = sb2("red", [128, 8], F32)
    sxf = sb2("sxf", [128, NQT, 16], F32)
    sx16 = sb2("sx16", [128, NQT, 16], I16)
    psA2 = psb2("psA2", [128, 512])
    psB2 = psb2("psB2", [128, 512])
    psAff = psb2("psAff", [64, 512])
    psB2b = ps2.enter_context(nc.psum_tensor("psB2b", [128, 512], BF16))

    p.dma("sync", sMisc, cent_sb[:, :], cent[:, :])
    t_e = p.dma("sync", sMisc, ebias_sb[:, :], ebias[:, :])
    p.wait("tensor", sMisc, t_e)
    for c in range(PD):
        p.op("tensor", lambda En, c=c: En.transpose(
            out=psA2[:, 0:64], in_=cent_sb[:, c * 128:(c + 1) * 128],
            identity=ident_f[0:64, 0:64]), sem=sT)
        p.wait("vector", sT, sT.n)
        p.op("vector", lambda En, c=c: En.tensor_copy(
            out=centT[:, c, :], in_=psA2[:, 0:64]), sem=sV)
        p.wait("tensor", sV, sV.n)
    p.op("tensor", lambda En: En.matmul(out=psA2[:, 0:64], lhsT=ones_f[:, :],
                                        rhs=ebias_sb[:, :], start=True, stop=True),
         sem=sT)
    p.wait("vector", sT, sT.n)
    p.op("vector", lambda En: En.tensor_copy(out=ebias_t[:, :], in_=psA2[:, 0:64]),
         sem=sV)
    p.wait("tensor", sV, sV.n)

    p2_state = {"xm_free": {}, "xmT_dma": None, "aff_done": 0}
    for bb in range(4):
        par = bb % 2
        if bb >= 2:
            p.wait("sync", sT, p2_state["xm_free"][bb - 2])
            p.wait("sync", sS, p2_state.get("xm_free_s", {}).get(bb - 2, 0))
        t_ld = p.dma("sync", sXb[par], xm[:, par, :, :],
                     xmoe[bb * 512:(bb + 1) * 512, :].rearrange(
                         "(a p) d -> p a d", p=128))
        p.wait("vector", sXb[par], t_ld)
        p.wait("scalar", sXb[par], t_ld)
        emit_ln_stats(xm[:, par, :, :], 4, stats2, sqb2, hstep=2)
        tst = sV.n
        p.wait("scalar", sV, tst)
        p.wait("scalar", sT, sT.n)
        for i in range(4):
            p.op("scalar", lambda En, par=par, i=i: En.activation(
                out=xn32[:, i, :], in_=xm[:, par, i, :], func=ACT.Identity,
                scale=stats2[:, 8 + i:9 + i], bias=stats2[:, 12 + i:13 + i]),
                sem=sS)
        tapply = sS.n
        p2_state.setdefault("xm_free_s", {})[bb] = sS.n
        p.wait("tensor", sS, tapply)
        p.wait("tensor", sT, p2_state["aff_done"])
        for i in range(4):
            p.wait("tensor", sV, sV.n)
            for c in range(PD):
                ps = psA2 if c < 4 else psB2
                p.op("tensor", lambda En, ps=ps, c=c, i=i: En.transpose(
                    out=ps[:, (c % 4) * 128:(c % 4 + 1) * 128],
                    in_=xn32[:, i, c * 128:(c + 1) * 128],
                    identity=ident_f[:, :]),
                    sem=sT if c % 4 == 3 else None)
            tt4 = sT.n
            p.wait("vector", sT, tt4)
            p.op("vector", lambda En, i=i: En.tensor_copy(
                out=xnT32[:, 0:4, i * 128:(i + 1) * 128],
                in_=psA2[:, :].rearrange("p (c t) -> p c t", c=4)), sem=sV)
            p.op("vector", lambda En, i=i: En.tensor_copy(
                out=xnT32[:, 4:8, i * 128:(i + 1) * 128],
                in_=psB2[:, :].rearrange("p (c t) -> p c t", c=4)), sem=sV)
            p.op("vector", lambda En, bb=bb, i=i: En.tensor_copy(
                out=xnT_moe[:, 0:4, (bb * 4 + i) * 128:(bb * 4 + i + 1) * 128],
                in_=psA2[:, :].rearrange("p (c t) -> p c t", c=4)), sem=sV)
            p.op("vector", lambda En, bb=bb, i=i: En.tensor_copy(
                out=xnT_moe[:, 4:8, (bb * 4 + i) * 128:(bb * 4 + i + 1) * 128],
                in_=psB2[:, :].rearrange("p (c t) -> p c t", c=4)), sem=sV)
        p2_state["xm_free"][bb] = sT.n
        p.wait("scalar", sSt, p2_state.get("xnm_dma", 0))
        p.op("scalar", lambda En: En.activation(
            out=xnmb[:, :, :].rearrange("p a d -> p (a d)"),
            in_=xn32[:, :, :].rearrange("p a d -> p (a d)"),
            func=ACT.Copy), sem=sS)
        p.dma("scalar", sSt,
              xnm_h[bb * 512:(bb + 1) * 512, :].rearrange("(a p) d -> p a d", p=128),
              xnmb[:, :, :])
        p2_state["xnm_dma"] = sSt.n
        p.wait("tensor", sV, sV.n)
        for c in range(PD):
            p.op("tensor", lambda En, c=c: En.matmul(
                out=psAff[:, :], lhsT=centT[:, c, :], rhs=xnT32[:, c, :],
                start=(c == 0), stop=(c == PD - 1)),
                sem=sT if c == PD - 1 else None)
        p2_state["aff_done"] = sT.n
        p.wait("vector", sT, sT.n)
        p.op("vector", lambda En, bb=bb: En.tensor_copy(
            out=affT[:, bb * 512:(bb + 1) * 512], in_=psAff[:, :]), sem=sV)
        p.wait("tensor", sV, sV.n)

    for tt in range(T // 128):
        p.op("tensor", lambda En, tt=tt: En.transpose(
            out=psA2[:, 0:64], in_=affT[:, tt * 128:(tt + 1) * 128],
            identity=ident_f[0:64, 0:64]), sem=sT)
        p.wait("vector", sT, sT.n)
        p.op("vector", lambda En: En.tensor_copy(out=aff_tok[:, :],
                                                 in_=psA2[:, 0:64]), sem=sV)
        p.op("vector", lambda En: En.tensor_add(out=bia[:, :], in0=aff_tok[:, :],
                                                in1=ebias_t[:, :]))
        p.op("vector", lambda En: En.tensor_copy(out=work[:, :], in_=bia[:, :]))
        for it in range(TOPK - 1):
            p.op("vector", lambda En: En.reduce_max(out=rst[:, 0:1], in_=work[:, :],
                                                    axis=AX))
            p.op("vector", lambda En: En.tensor_single_scalar(
                out=mtile[:, :], in_=work[:, :], scalar=rst[:, 0:1], op=ALU.is_ge))
            if it == 0:
                p.op("vector", lambda En: En.tensor_copy(out=macc[:, :],
                                                         in_=mtile[:, :]))
            else:
                p.op("vector", lambda En: En.tensor_add(
                    out=macc[:, :], in0=macc[:, :], in1=mtile[:, :]))
            if it == 1:
                p.op("vector", lambda En: En.tensor_copy(out=rnk[:, :],
                                                         in_=mtile[:, :]))
            elif it >= 2:
                p.op("vector", lambda En, it=it: En.scalar_tensor_tensor(
                    out=rnk[:, :], in0=mtile[:, :], scalar=C_IT[it], in1=rnk[:, :],
                    op0=ALU.mult, op1=ALU.add))
            p.op("vector", lambda En: En.scalar_tensor_tensor(
                out=work[:, :], in0=mtile[:, :], scalar=C_BIG, in1=work[:, :],
                op0=ALU.mult, op1=ALU.add))
        p.op("vector", lambda En: En.reduce_max(out=rst[:, 0:1], in_=work[:, :],
                                                axis=AX))
        p.op("vector", lambda En: En.tensor_single_scalar(
            out=mtile[:, :], in_=bia[:, :], scalar=rst[:, 0:1], op=ALU.is_ge),
            sem=sV)
        p.wait("scalar", sV, sV.n)
        p.op("scalar", lambda En: En.activation(out=sig[:, :], in_=aff_tok[:, :],
                                                func=ACT.Sigmoid), sem=sS)
        p.wait("vector", sS, sS.n)
        p.op("vector", lambda En: En.tensor_mul(out=sig[:, :], in0=sig[:, :],
                                                in1=mtile[:, :]))
        p.op("vector", lambda En: En.reduce_sum(out=rst[:, 1:2], in_=sig[:, :],
                                                axis=AX))
        p.op("vector", lambda En: En.tensor_scalar_add(out=rst[:, 1:2],
                                                       in0=rst[:, 1:2],
                                                       scalar1=C_E8), sem=sV)
        p.wait("vector", sV)
        p.op("vector", lambda En: En.reciprocal(out=rst[:, 2:3], in_=rst[:, 1:2]),
             sem=sV)
        p.wait("vector", sV)
        p.op("vector", lambda En: En.tensor_scalar_mul(out=sig[:, :], in0=sig[:, :],
                                                       scalar1=rst[:, 2:3]))
        p.op("vector", lambda En: En.tensor_copy(out=cwb[:, :], in_=sig[:, :]),
             sem=sV)
        p.op("vector", lambda En: En.tensor_sub(out=macc[:, :], in0=mtile[:, :],
                                                in1=macc[:, :]))
        p.op("vector", lambda En: En.scalar_tensor_tensor(
            out=rnk[:, :], in0=macc[:, :], scalar=C_I7, in1=rnk[:, :],
            op0=ALU.mult, op1=ALU.add))
        p.op("vector", lambda En, tt=tt: En.scalar_tensor_tensor(
            out=rankT[:, tt, :], in0=rnk[:, :], scalar=C_ONE, in1=mtile[:, :],
            op0=ALU.add, op1=ALU.mult))
        p.op("vector", lambda En, tt=tt: En.tensor_scalar_add(
            out=rankT[:, tt, :], in0=rankT[:, tt, :], scalar1=C_NEG))
        p.wait("tensor", sV, sV.n)
        p.op("tensor", lambda En, tt=tt: En.transpose(
            out=psB2b[0:64, 0:128], in_=cwb[:, :], identity=ident_b[:, :]), sem=sT)
        p.wait("vector", sT, sT.n)
        p.op("vector", lambda En, tt=tt: En.tensor_copy(
            out=cwT[0:64, tt * 128:(tt + 1) * 128], in_=psB2b[0:64, 0:128]),
            sem=sV)
        p.wait("tensor", sV, sV.n)

    if "dbg_cwt" in dbg:
        p.wait("gpsimd", sV, sV.n)
        p.dma("gpsimd", sSt2, dbg["dbg_cwt"][:, :], cwT[:, :])
    if "dbg_aff" in dbg:
        p.wait("gpsimd", sV, sV.n)
        p.dma("gpsimd", sSt2, dbg["dbg_aff"][:, :], affT[:, :])

    t_ecb = p.dma("sync", sMisc, ecb[:, :], ecb_h[:, :])
    v, g = "vector", "gpsimd"
    p.op(v, lambda En: En.tensor_single_scalar(
        out=maskT[:, :], in_=cwT[:, :], scalar=0.0, op=ALU.is_gt))
    p.op(v, lambda En: En.memset(onesET[:, :], 1.0), sem=sV)
    p.op(v, lambda En: En.tensor_tensor_scan(
        out=sposc[:, :], data0=onesET[:, :], data1=maskT[:, :],
        initial=0.0, op0=ALU.mult, op1=ALU.add))
    p.op(v, lambda En: En.tensor_sub(out=sposc[:, :], in0=sposc[:, :],
                                     in1=maskT[:, :]))
    p.op(v, lambda En: En.tensor_scalar(out=sposc[:, :], in0=sposc[:, :],
                                        scalar1=float(CAP - 1), scalar2=None,
                                        op0=ALU.min))
    p.op(v, lambda En: En.tensor_scalar(out=sidxf[:, :], in0=sposc[:, :],
                                        scalar1=1.0, scalar2=None, op0=ALU.add))
    p.op(v, lambda En: En.tensor_tensor(out=sidxf[:, :], in0=sidxf[:, :],
                                        in1=maskT[:, :], op=ALU.mult))
    p.op(v, lambda En: En.tensor_scalar(out=sidxf[:, :], in0=sidxf[:, :],
                                        scalar1=-1.0, scalar2=None, op0=ALU.add))
    p.op(v, lambda En: En.tensor_copy(out=sidx16[:, :], in_=sidxf[:, :]),
         sem=sV)
    t_sidx = sV.n
    p.op(g, lambda En: En.iota(iota16[:, :], [[1, T]], base=0,
                               channel_multiplier=0,
                               allow_small_or_imprecise_dtypes=True))
    p.op(g, lambda En: En.iota(wrmap16[:, :], [[1, CAP // 16], [CAP // 16, 16]],
                               base=0, channel_multiplier=0,
                               allow_small_or_imprecise_dtypes=True))
    p.wait(g, sV, t_sidx)
    p.op(g, lambda En: En.local_scatter(
        tok_l[:, :], iota16[:, :], sidx16[:, :],
        channels=E, num_elems=CAP, num_idxs=T))
    p.op(g, lambda En: En.local_scatter(
        gates_l[:, :], cwT[:, :], sidx16[:, :],
        channels=E, num_elems=CAP, num_idxs=T))
    p.op(g, lambda En: En.local_scatter(
        tok_lw[:, :], tok_l[:, :], wrmap16[:, :],
        channels=E, num_elems=CAP, num_idxs=CAP), sem=sG)
    t_lsc = sG.n
    p.wait(v, sG, t_lsc)
    p.op(v, lambda En: En.tensor_copy(out=gates32[:, :], in_=gates_l[:, :]),
         sem=sV)
    t_g32 = sV.n
    p.wait("tensor", sV, t_g32)
    for sc_ in range(CAP // 128):
        p.op("tensor", lambda En, sc_=sc_: En.transpose(
            out=psA2[:, 0:64], in_=gates32[:, sc_ * 128:(sc_ + 1) * 128],
            identity=ident_f[0:64, 0:64]), sem=sT)
        p.wait(v, sT, sT.n)
        p.op(v, lambda En, sc_=sc_: En.tensor_copy(
            out=gatesc[:, sc_, :], in_=psA2[:, 0:64]), sem=sV)
        p.wait("tensor", sV, sV.n)
    p.wait("sync", sG, t_lsc)
    t_th = p.dma("sync", sIx, tok_h[:, :], tok_lw[:, :])
    p.wait("sync", sIx, t_th)
    for k in range(8):
        p.dma("sync", sIx, idx_all[16 * k:16 * (k + 1), :, :],
              tok_h.rearrange("e (q f) -> q e f", q=16))

    p.wait("tensor", sMisc, t_ecb)
    p.wait("tensor", sV, sV.n)
    for tt in range(NQT):
        p.wait("tensor", sV, sV.n)
        p.op("tensor", lambda En, tt=tt: En.transpose(
            out=psA2[:, 0:64], in_=sposc[:, tt * 128:(tt + 1) * 128],
            identity=ident_f[0:64, 0:64]), sem=sT)
        p.wait(v, sT, sT.n)
        p.op(v, lambda En: En.tensor_add(out=spT2[:, :], in0=psA2[:, 0:64],
                                         in1=ecb[:, :]), sem=sV)
        for j in range(8):
            p.op(v, lambda En, tt=tt, j=j: En.tensor_single_scalar(
                out=mrank[:, :], in_=rankT[:, tt, :], scalar=float(j),
                op=ALU.is_equal))
            p.op(v, lambda En: En.tensor_mul(out=mt0[:, :], in0=mrank[:, :],
                                             in1=spT2[:, :]))
            p.op(v, lambda En: En.reduce_sum(out=red[:, 0:1],
                                             in_=mt0[:, 0:EH], axis=AX))
            p.op(v, lambda En: En.reduce_sum(out=red[:, 1:2],
                                             in_=mt0[:, EH:E], axis=AX))
            p.op(v, lambda En: En.reduce_sum(out=red[:, 2:3],
                                             in_=mrank[:, 0:EH], axis=AX))
            p.op(v, lambda En, tt=tt, j=j: En.scalar_tensor_tensor(
                out=sxf[:, tt, 2 * j + 1:2 * j + 2], in0=red[:, 2:3], scalar=C_Z,
                in1=red[:, 1:2], op0=ALU.mult, op1=ALU.add))
            p.op(v, lambda En: En.scalar_tensor_tensor(
                out=red[:, 3:4], in0=red[:, 2:3], scalar=C_NZ,
                in1=red[:, 0:1], op0=ALU.mult, op1=ALU.add))
            p.op(v, lambda En, tt=tt, j=j: En.tensor_scalar_add(
                out=sxf[:, tt, 2 * j:2 * j + 1], in0=red[:, 3:4], scalar1=C_Z))
    p.op(v, lambda En: En.tensor_copy(out=sx16[:, :, :], in_=sxf[:, :, :]),
         sem=sV)
    t_sx = sV.n
    p.wait("sync", sV, t_sx)
    t_sh = p.dma("sync", sIx, sxtok_h.rearrange("(t p) w -> p t w", p=128),
                 sx16[:, :, :])
    p.wait("sync", sIx, t_sh)
    t_i5 = p.dma("sync", sIx, idxrank5[:, :, :, :],
                 sxtok_h.rearrange("(f q) (j h) -> q f j h", q=16, j=8))
    p.wait(v, sIx, t_i5)
    for j in range(8):
        for h in range(2):
            p.op(v, lambda En, j=j, h=h: En.tensor_copy(
                out=idxrank[0:16, j, h, :], in_=idxrank5[:, :, j, h]),
                sem=sV)
    p.wait("sync", sV, sV.n)
    t_ir = 0
    for k in range(1, 8):
        t_ir = p.dma("sync", sIx, idxrank[16 * k:16 * (k + 1), :, :, :],
                     idxrank[0:16, :, :, :])

    if "dbg_tok" in dbg:
        p.wait("gpsimd", sG, sG.n)
        p.dma("gpsimd", sSt2, dbg["dbg_tok"][:, :], tok_l[:, :])
    if "dbg_gate" in dbg:
        p.wait("gpsimd", sG, sG.n)
        p.dma("gpsimd", sSt2, dbg["dbg_gate"][:, :], gates_l[:, :])
    if "dbg_rank" in dbg:
        p.wait("gpsimd", sV, sV.n)
        p.dma("gpsimd", sSt2, dbg["dbg_rank"][:, :, :], rankT[:, :, :])
    if "dbg_sidx" in dbg:
        p.wait("gpsimd", sV, sV.n)
        p.dma("gpsimd", sSt2, dbg["dbg_sidx"][:, :, :], sxf[:, :, :])

    p.barrier(ALL_SEMS)
    ps2.close()
    st_p2.close()

    st_p3a = contextlib.ExitStack()
    sb3a = lambda name, shape, dt: st_p3a.enter_context(nc.sbuf_tensor(name, shape, dt))
    ps3 = contextlib.ExitStack()
    psb3 = lambda name, shape: ps3.enter_context(nc.psum_tensor(name, shape, F32))
    psP = [psb3(f"psP{i}", [128, 512]) for i in range(8)]

    wsupb = sb3a("wsupb", [128, PD, F], BF16)
    wsdnb = sb3a("wsdnb", [128, 4, D], BF16)
    wst3 = sb3a("wst3", [128, PD, F], F32)
    gts = sb3a("gts", [128, 4, T], BF16)
    ysh = sb3a("ysh", [128, 2, 4, D], F32)

    t1 = p.dma("sync", sWst, wst3[:, :, :],
               wsup.rearrange("(c p) f -> p c f", p=128))
    p.wait("vector", sWst, t1)
    p.op("vector", lambda En: En.tensor_copy(
        out=wsupb[:, :, :].rearrange("p c f -> p (c f)"),
        in_=wst3[:, :, :].rearrange("p c f -> p (c f)")), sem=sV)
    t_wsup = sV.n
    p.wait("sync", sV, t_wsup)
    t2 = p.dma("sync", sWst,
               wst3[:, :, :].rearrange("p c f -> p (c f)").rearrange(
                   "p (c d) -> p c d", c=4),
               wsdn.rearrange("(c p) d -> p c d", p=128))
    p.wait("vector", sWst, t2)
    p.op("vector", lambda En: En.tensor_copy(
        out=wsdnb[:, :, :].rearrange("p c d -> p (c d)"),
        in_=wst3[:, :, :].rearrange("p c f -> p (c f)")), sem=sV)
    t_wsdn = sV.n

    p.wait("tensor", sV, t_wsup)
    sil_ev = {i: 0 for i in range(4)}
    for tg in range(4):
        for mf in range(4):
            p.wait("tensor", sS, sil_ev[mf])
            for c in range(PD):
                p.op("tensor", lambda En, mf=mf, c=c, tg=tg: En.matmul(
                    out=psP[mf][:, :],
                    lhsT=wsupb[:, c, mf * 128:(mf + 1) * 128],
                    rhs=xnT_moe[:, c, tg * 512:(tg + 1) * 512],
                    start=(c == 0), stop=(c == PD - 1)),
                    sem=sT if c == PD - 1 else None)
            tmm = sT.n
            p.wait("scalar", sT, tmm)
            p.op("scalar", lambda En, mf=mf, tg=tg: En.activation(
                out=gts[:, mf, tg * 512:(tg + 1) * 512], in_=psP[mf][:, :],
                func=ACT.Silu), sem=sS)
            sil_ev[mf] = sS.n
    t_gts = sS.n
    p.wait("tensor", sS, t_gts)
    p.wait("tensor", sV, t_wsdn)
    dn_ev = {i: 0 for i in range(4, 8)}
    ysh_st = {0: 0, 1: 0}
    for blk in range(4):
        par = blk % 2
        p.wait("scalar", sYs, ysh_st[par])
        for a in range(4):
            sc = blk * 4 + a
            for dh in range(2):
                bank = 4 + ((sc * 2 + dh) % 4)
                p.wait("tensor", sS, dn_ev[bank])
                for cf in range(4):
                    p.op("tensor", lambda En, bank=bank, cf=cf, sc=sc, dh=dh:
                         En.matmul(
                             out=psP[bank][:, :],
                             lhsT=gts[:, cf, sc * 128:(sc + 1) * 128],
                             rhs=wsdnb[:, cf, dh * 512:(dh + 1) * 512],
                             start=(cf == 0), stop=(cf == 3)),
                         sem=sT if cf == 3 else None)
                tmm = sT.n
                p.wait("scalar", sT, tmm)
                p.op("scalar", lambda En, par=par, a=a, dh=dh, bank=bank:
                     En.activation(out=ysh[:, par, a, dh * 512:(dh + 1) * 512],
                                   in_=psP[bank][:, :], func=ACT.Copy), sem=sS)
                dn_ev[bank] = sS.n
        p.wait("sync", sS, sS.n)
        p.dma("sync", sYs,
              ysh_h[blk * 512:(blk + 1) * 512, :].rearrange(
                  "(a p) d -> p a d", p=128),
              ysh[:, par, :, :])
        ysh_st[par] = sYs.n

    if "dbg_ysh" in dbg:
        p.wait("gpsimd", sYs, sYs.n)
        p.dma("gpsimd", sSt2, dbg["dbg_ysh"][:, :], ysh_h[:, :])

    p.barrier(ALL_SEMS)
    ps3.close()
    st_p3a.close()
    st_moe.close()

    st_p3b = contextlib.ExitStack()
    sb3 = lambda name, shape, dt: st_p3b.enter_context(nc.sbuf_tensor(name, shape, dt))
    ps3 = contextlib.ExitStack()
    psb3 = lambda name, shape: ps3.enter_context(nc.psum_tensor(name, shape, F32))
    psP = [psb3(f"psQ{i}", [128, 512]) for i in range(8)]

    wupb = sb3("wupb", [128, 2, PD, F], BF16)
    wdnb = sb3("wdnb", [128, 2, 4, D], BF16)
    wst = sb3("wst", [128, PD, F], F32)
    xg = sb3("xg", [128, 3, PD, CAP], BF16)
    gt = sb3("gt", [128, 2, 4, CAP], BF16)
    ytok = sb3("ytok", [128, 2, CAP // 128, D], BF16)
    zrow = sb3("zrow", [1, D], BF16)

    p.op("gpsimd", lambda En: En.memset(zrow[:, :], 0.0), sem=sG)
    p.wait("sync", sG, sG.n)
    p.dma("sync", sLd, yall_h[ZROW:ZROW + 1, :], zrow[:, :])
    p.dma("sync", sLd, yall_h[NHALF + ZROW:NHALF + ZROW + 1, :], zrow[:, :])

    wst_free = [0]
    up_ready, dn_ready, ga = {}, {}, {}
    up_user = {0: 0, 1: 0}
    dn_user = {0: 0, 1: 0}
    gt_free = {0: 0, 1: 0}
    up_done = {}
    ytok_st = {0: 0, 1: 0}
    sil_ev = {i: 0 for i in range(4)}
    dn_ev = {i: 0 for i in range(4, 8)}

    def eload_up(e, slot):
        p.wait("sync", sV, wst_free[0])
        t1 = p.dma("sync", sWst, wst[:, :, :],
                   wup[e].rearrange("(c p) f -> p c f", p=128))
        p.wait("vector", sWst, t1)
        p.wait("vector", sT, up_user[slot])
        p.op("vector", lambda En, slot=slot: En.tensor_copy(
            out=wupb[:, slot, :, :].rearrange("p c f -> p (c f)"),
            in_=wst[:, :, :].rearrange("p c f -> p (c f)")), sem=sV)
        up_ready[e] = wst_free[0] = sV.n

    def eload_dn(e, slot):
        p.wait("sync", sV, wst_free[0])
        t1 = p.dma("sync", sWst,
                   wst[:, :, :].rearrange("p c f -> p (c f)").rearrange(
                       "p (c d) -> p c d", c=4),
                   wdn[e].rearrange("(c p) d -> p c d", p=128))
        p.wait("vector", sWst, t1)
        p.wait("vector", sT, dn_user[slot])
        p.op("vector", lambda En, slot=slot: En.tensor_copy(
            out=wdnb[:, slot, :, :].rearrange("p c d -> p (c d)"),
            in_=wst[:, :, :].rearrange("p c f -> p (c f)")), sem=sV)
        dn_ready[e] = wst_free[0] = sV.n

    def egather(e):
        buf = e % 3
        if e >= 3:
            p.wait("gpsimd", sT, up_done[e - 3])
        ga[e] = p.opdma("gpsimd", sGa, lambda En, e=e, buf=buf: En.dma_gather(
            xg[:, buf, :, :], xnm_h[:, :], idx_all[:, e, :],
            num_idxs=CAP, num_idxs_reg=CAP, elem_size=D, elem_step=D,
            transpose=True, queue_num=0))

    eload_up(0, 0)
    eload_dn(0, 0)
    egather(0)
    egather(1)
    egather(2)
    NSC = CAP // 128
    for e in range(E):
        slot = e % 2
        buf = e % 3
        if e + 1 < E:
            eload_up(e + 1, (e + 1) % 2)
        p.wait("tensor", sV, up_ready[e])
        p.wait("tensor", sGa, ga[e])
        p.wait("scalar", sT, gt_free[slot])
        for s0, sl in ((0, 512), (512, 256)):
            for mf in range(4):
                p.wait("tensor", sS, sil_ev[mf])
                for c in range(PD):
                    p.op("tensor",
                         lambda En, mf=mf, c=c, s0=s0, sl=sl, slot=slot, buf=buf:
                         En.matmul(
                             out=psP[mf][:, 0:sl],
                             lhsT=wupb[:, slot, c, mf * 128:(mf + 1) * 128],
                             rhs=xg[:, buf, c, s0:s0 + sl],
                             start=(c == 0), stop=(c == PD - 1)),
                         sem=sT if c == PD - 1 else None)
                tmm = sT.n
                p.wait("scalar", sT, tmm)
                p.op("scalar", lambda En, mf=mf, s0=s0, sl=sl, slot=slot:
                     En.activation(out=gt[:, slot, mf, s0:s0 + sl],
                                   in_=psP[mf][:, 0:sl], func=ACT.Silu), sem=sS)
                sil_ev[mf] = sS.n
        up_done[e] = up_user[slot] = sT.n
        t_gt = sS.n
        if e + 1 < E:
            eload_dn(e + 1, (e + 1) % 2)
        if e + 3 < E:
            egather(e + 3)
        p.wait("tensor", sS, t_gt)
        p.wait("tensor", sV, dn_ready[e])
        p.wait("scalar", sYs, ytok_st[slot])
        for sc in range(NSC):
            for dh in range(2):
                bank = 4 + ((sc * 2 + dh) % 4)
                p.wait("tensor", sS, dn_ev[bank])
                for cf in range(4):
                    p.op("tensor",
                         lambda En, bank=bank, cf=cf, sc=sc, dh=dh, slot=slot:
                         En.matmul(
                             out=psP[bank][:, :],
                             lhsT=gt[:, slot, cf, sc * 128:(sc + 1) * 128],
                             rhs=wdnb[:, slot, cf, dh * 512:(dh + 1) * 512],
                             start=(cf == 0), stop=(cf == 3)),
                         sem=sT if cf == 3 else None)
                tmm = sT.n
                p.wait("scalar", sT, tmm)
                p.op("scalar",
                     lambda En, slot=slot, sc=sc, dh=dh, bank=bank, e=e:
                     En.activation(
                         out=ytok[:, slot, sc, dh * 512:(dh + 1) * 512],
                         in_=psP[bank][:, :], func=ACT.Copy,
                         scale=gatesc[:, sc, e:e + 1]), sem=sS)
                dn_ev[bank] = sS.n
        dn_user[slot] = gt_free[slot] = sT.n
        base = e * CAP if e < EH else NHALF + (e - EH) * CAP
        p.dma("scalar", sYs,
              yall_h[base:base + CAP, :].rearrange("(sc p) d -> p sc d", p=128),
              ytok[:, slot, :, :])
        ytok_st[slot] = sYs.n

    if "dbg_yall" in dbg:
        p.wait("gpsimd", sYs, sYs.n)
        p.dma("gpsimd", sSt2, dbg["dbg_yall"][:, :], yall_h[:, :])

    p.barrier(ALL_SEMS)
    ps3.close()
    st_p3b.close()
    st_tab.close()

    st_p4 = contextlib.ExitStack()
    sb4 = lambda name, shape, dt: st_p4.enter_context(nc.sbuf_tensor(name, shape, dt))
    ps4 = contextlib.ExitStack()
    psb4 = lambda name, shape: ps4.enter_context(nc.psum_tensor(name, shape, F32))

    kbuf = sb4("kbuf", [128, 2, PD, 2048], BF16)
    vbuf = sb4("vbuf", [128, 2, 16, D], BF16)
    qtile = sb4("qtile", [128, 2, PD, 128], BF16)
    acc = sb4("acc", [128, 8, D], F32)
    sc = sb4("sc", [128, 2, 2048], F32)
    pb = sb4("pb", [128, 2, 2048], BF16)
    ptb = sb4("ptb", [128, 16, 128], BF16)
    astat = sb4("astat", [128, NQT, 6], F32)
    obf = sb4("obf", [128, D], BF16)
    psS4 = [psb4(f"psS4{i}", [128, 512]) for i in range(2)]
    psT4 = [ps4.enter_context(nc.psum_tensor(f"psT4{i}", [128, 512], BF16))
            for i in range(4)]
    psPV = [psb4(f"psPV{i}", [128, 512]) for i in range(2)]

    kv_used = {}
    sc_free = {0: 0, 1: 0}
    pb_free = {0: 0, 1: 0}
    st4 = {"pv_done": 0, "prev_tsc": 0, "q_used": {}}

    def scores_part(qg, m, qi, tk):
        q = qg * 8 + qi
        par = m % 2
        qpar = qi % 2
        spar = qi % 2
        key = (qg, m, qi - 2)
        if key in st4["q_used"]:
            p.wait("gpsimd", sT, st4["q_used"][key])
        elif qi < 2 and (qg, m) != (0, 0):
            pkey = ((qg, m - 1, 6 + qi) if m > 0 else (qg - 1, NMC - 1, 6 + qi))
            p.wait("gpsimd", sT, st4["q_used"][pkey])
        tq = p.dma("gpsimd", sQt[qpar], qtile[:, qpar, :, :],
                   qt_h[:, :, q * 128:(q + 1) * 128].rearrange("c p t -> p c t"))
        p.wait("tensor", sKV[par], tk)
        p.wait("tensor", sQt[qpar], tq)
        p.wait("tensor", sS, st4["prev_tsc"])
        for half in range(2):
            if half == 0:
                p.wait("tensor", sS, sc_free[spar])
            else:
                p.wait("tensor", sS, sS.n)
            for c in range(PD):
                for tkb in range(2):
                    p.op("tensor", lambda En, c=c, half=half, tkb=tkb,
                         qpar=qpar, par=par: En.matmul(
                             out=psS4[tkb][:, :],
                             lhsT=qtile[:, qpar, c, :],
                             rhs=kbuf[:, par, c,
                                      (half * 2 + tkb) * 512:
                                      (half * 2 + tkb + 1) * 512],
                             start=(c == 0), stop=(c == PD - 1)),
                         sem=sT if (c == PD - 1 and tkb == 1) else None)
            ts = sT.n
            p.wait("scalar", sT, ts)
            for tkb in range(2):
                p.op("scalar", lambda En, half=half, tkb=tkb, spar=spar:
                     En.copy(out=sc[:, spar, (half * 2 + tkb) * 512:
                                    (half * 2 + tkb + 1) * 512],
                             in_=psS4[tkb][:, :]), sem=sS)
        st4["q_used"][(qg, m, qi)] = sT.n
        st4["prev_tsc"] = sS.n
        return sS.n

    def softpv_part(qg, m, qi, tsc, tv):
        q = qg * 8 + qi
        par = m % 2
        spar = qi % 2
        p.wait("vector", sS, tsc)
        p.op("vector", lambda En, spar=spar, q=q: En.reduce_max(
            out=astat[:, q, 2:3], in_=sc[:, spar, :], axis=AX))
        if m == 0:
            p.op("vector", lambda En, q=q: En.tensor_copy(
                out=astat[:, q, 0:1], in_=astat[:, q, 2:3]))
        else:
            p.op("vector", lambda En, q=q: En.tensor_max(
                out=astat[:, q, 4:5], in0=astat[:, q, 0:1],
                in1=astat[:, q, 2:3]))
            p.op("vector", lambda En, q=q: En.tensor_sub(
                out=astat[:, q, 5:6], in0=astat[:, q, 0:1],
                in1=astat[:, q, 4:5]))
            p.op("vector", lambda En, q=q: En.tensor_copy(
                out=astat[:, q, 0:1], in_=astat[:, q, 4:5]))
        p.op("vector", lambda En, q=q: En.tensor_scalar_mul(
            out=astat[:, q, 3:4], in0=astat[:, q, 0:1], scalar1=C_NEG), sem=sV)
        tmax = sV.n
        p.wait("scalar", sV, tmax)
        if m > 0:
            p.op("scalar", lambda En, q=q: En.activation(
                out=astat[:, q, 2:3], in_=astat[:, q, 5:6], func=ACT.Exp))
        p.wait("scalar", sT, pb_free[spar])
        p.op("scalar", lambda En, spar=spar, q=q: En.activation(
            out=pb[:, spar, :], in_=sc[:, spar, :], func=ACT.Exp,
            bias=astat[:, q, 3:4]), sem=sS)
        texp = sS.n
        sc_free[spar] = texp
        p.wait("vector", sS, texp)
        p.op("vector", lambda En, spar=spar, q=q: En.reduce_sum(
            out=astat[:, q, 4:5], in_=pb[:, spar, :], axis=AX))
        if m == 0:
            p.op("vector", lambda En, q=q: En.tensor_copy(
                out=astat[:, q, 1:2], in_=astat[:, q, 4:5]), sem=sV)
        else:
            p.op("vector", lambda En, q=q: En.scalar_tensor_tensor(
                out=astat[:, q, 1:2], in0=astat[:, q, 1:2],
                scalar=astat[:, q, 2:3], in1=astat[:, q, 4:5],
                op0=ALU.mult, op1=ALU.add), sem=sV)
        p.wait("tensor", sKV[par], tv)
        p.wait("tensor", sS, texp)
        p.wait("tensor", sV, st4["pv_done"])
        for g4 in range(4):
            p.wait("tensor", sS, sS.n)
            for k in range(4):
                kk = g4 * 4 + k
                p.op("tensor", lambda En, g4=g4, k=k, kk=kk, spar=spar:
                     En.transpose(out=psT4[g4][:, k * 128:(k + 1) * 128],
                                  in_=pb[:, spar, kk * 128:(kk + 1) * 128],
                                  identity=ident_b[:, :]),
                     sem=sT if k == 3 else None)
            ttr = sT.n
            p.wait("scalar", sT, ttr)
            p.op("scalar", lambda En, g4=g4: En.copy(
                out=ptb[:, g4 * 4:(g4 + 1) * 4, :].rearrange("p a t -> p (a t)"),
                in_=psT4[g4][:, :]), sem=sS)
            tpt = sS.n
            p.wait("tensor", sS, tpt)
            for k in range(4):
                kk = g4 * 4 + k
                for half in range(2):
                    p.op("tensor", lambda En, kk=kk, half=half, par=par:
                         En.matmul(
                             out=psPV[half][:, :],
                             lhsT=ptb[:, kk, :],
                             rhs=vbuf[:, par, kk, half * 512:(half + 1) * 512],
                             start=(kk == 0), stop=(kk == 15)),
                         sem=sT if (kk == 15 and half == 1) else None)
        pb_free[spar] = sT.n
        tpv = sT.n
        p.wait("vector", sT, tpv)
        for half in range(2):
            if m == 0:
                p.op("vector", lambda En, half=half, qi=qi: En.tensor_copy(
                    out=acc[:, qi, half * 512:(half + 1) * 512],
                    in_=psPV[half][:, :]), sem=sV)
            else:
                p.op("vector", lambda En, half=half, qi=qi, q=q:
                     En.scalar_tensor_tensor(
                         out=acc[:, qi, half * 512:(half + 1) * 512],
                         in0=acc[:, qi, half * 512:(half + 1) * 512],
                         scalar=astat[:, q, 2:3],
                         in1=psPV[half][:, :],
                         op0=ALU.mult, op1=ALU.add), sem=sV)
        st4["pv_done"] = sV.n

    for qg in range(2):
        for m in range(NMC):
            par = m % 2
            if m >= 2 or qg > 0:
                key = (qg, m - 2) if m >= 2 else (qg - 1, NMC - 2 + m)
                p.wait("sync", sT, kv_used[key])
            tk = p.dma("sync", sKV[par], kbuf[:, par, :, :],
                       kt_h[:, :, m * 2048:(m + 1) * 2048].rearrange(
                           "c p t -> p c t"))
            tv = p.dma("sync", sKV[par], vbuf[:, par, :, :],
                       v_h[m * 16:(m + 1) * 16, :, :].rearrange("tt p e -> p tt e"))
            tscs = {}
            tscs[0] = scores_part(qg, m, 0, tk)
            tscs[1] = scores_part(qg, m, 1, tk)
            for qi in range(8):
                softpv_part(qg, m, qi, tscs[qi], tv)
                if qi + 2 < 8:
                    tscs[qi + 2] = scores_part(qg, m, qi + 2, tk)
            kv_used[(qg, m)] = sT.n
        p.wait("vector", sV, sV.n)
        for qi in range(8):
            q = qg * 8 + qi
            p.wait("vector", sT, kv_used[(qg, NMC - 1)])
            p.op("vector", lambda En, q=q: En.reciprocal(
                out=astat[:, q, 5:6], in_=astat[:, q, 1:2]), sem=sV)
            p.wait("vector", sV)
            p.wait("vector", sSt, sSt.n)
            p.op("vector", lambda En, qi=qi, q=q: En.tensor_scalar_mul(
                out=obf[:, :], in0=acc[:, qi, :], scalar1=astat[:, q, 5:6]),
                sem=sV)
            p.wait("scalar", sV, sV.n)
            p.dma("scalar", sSt, out_h[q, :, :], obf[:, :])
        p.wait("sync", sSt, sSt.n)

    if "dbg_out" in dbg:
        p.wait("gpsimd", sSt, sSt.n)
        p.dma("gpsimd", sSt2, dbg["dbg_out"][:, :, :], out_h[:, :, :])

    p.barrier(ALL_SEMS)
    ps4.close()
    st_p4.close()

    st_p5 = contextlib.ExitStack()
    sb5 = lambda name, shape, dt: st_p5.enter_context(nc.sbuf_tensor(name, shape, dt))
    ps5 = contextlib.ExitStack()
    psb5 = lambda name, shape: ps5.enter_context(nc.psum_tensor(name, shape, F32))

    woT = sb5("woT", [128, PD * PD, 128], BF16)
    wld5 = sb5("wld5", [128, 2, D], F32)
    obuf = sb5("obuf", [128, 8, D], BF16)
    facc = sb5("facc", [128, 8, D], F32)
    yg = sb5("yg", [128, 2, 8, D], BF16)
    xmt = sb5("xmt", [128, 2, D], F32)
    finalT = sb5("finalT", [128, PD, D], F32)
    ybuf = sb5("ybuf", [128, 2, D], F32)
    psA5 = psb5("psA5", [128, 512])
    psB5 = psb5("psB5", [128, 512])
    psC5 = psb5("psC5", [128, 512])
    psD5 = psb5("psD5", [128, 512])

    load_wT(wo, woT, wld5, psA5, psB5)

    cl_done = 0
    for cl in range(2):
        if cl > 0:
            p.wait("sync", sT, cl_done)
            p.wait("gpsimd", sV, cl_done_v)
        tob = p.dma("sync", sMisc, obuf[:, :, :],
                    out_h[cl * 8:(cl + 1) * 8, :, :].rearrange("k p e -> p k e"))
        t_ysh = p.dma("gpsimd", sRs, facc[:, :, :],
                      ysh_h[cl * 1024:(cl + 1) * 1024, :].rearrange(
                          "(g p) d -> p g d", p=128))
        yg_cons = {0: 0, 1: 0}
        for i5 in range(16):
            j, h = i5 // 2, i5 % 2
            par5 = i5 % 2
            if i5 >= 2:
                p.wait("gpsimd", sV, yg_cons[par5])
            t_g = p.opdma("gpsimd", sGa,
                          lambda En, j=j, h=h, par5=par5, cl=cl: En.dma_gather(
                              yg[:, par5, :, :],
                              yall_h[h * NHALF:(h + 1) * NHALF, :],
                              idxrank[:, j, h, cl * 64:(cl + 1) * 64],
                              num_idxs=1024, num_idxs_reg=1024,
                              elem_size=D, elem_step=D, transpose=False,
                              queue_num=0))
            if i5 == 0:
                p.wait("vector", sRs, t_ysh)
            p.wait("vector", sGa, t_g)
            for q4 in range(2):
                p.op("vector", lambda En, par5=par5, q4=q4: En.tensor_add(
                    out=facc[:, q4 * 4:(q4 + 1) * 4, :].rearrange(
                        "p g d -> p (g d)"),
                    in0=facc[:, q4 * 4:(q4 + 1) * 4, :].rearrange(
                        "p g d -> p (g d)"),
                    in1=yg[:, par5, q4 * 4:(q4 + 1) * 4, :].rearrange(
                        "p g d -> p (g d)")), sem=sV)
            yg_cons[par5] = sV.n
        p.wait("tensor", sMisc, tob)
        for om in range(PD):
            p.wait("tensor", sV, sV.n)
            for half in range(2):
                ps = psA5 if half == 0 else psB5
                for k in range(PD):
                    p.op("tensor", lambda En, om=om, half=half, k=k, ps=ps:
                         En.matmul(out=ps[:, :], lhsT=woT[:, k * PD + om, :],
                                   rhs=obuf[:, k, half * 512:(half + 1) * 512],
                                   start=(k == 0), stop=(k == PD - 1)),
                         sem=sT if k == PD - 1 else None)
            tmm = sT.n
            p.wait("vector", sT, tmm)
            for half in range(2):
                ps = psA5 if half == 0 else psB5
                p.op("vector", lambda En, om=om, half=half, ps=ps: En.tensor_copy(
                    out=finalT[:, om, half * 512:(half + 1) * 512],
                    in_=ps[:, :]), sem=sV)
        tfin = sV.n
        p.wait("tensor", sV, tfin)
        ystore = {0: 0, 1: 0}
        for et in range(PD):
            par = et % 2
            p.wait("vector", sY[par], ystore[par])
            p.wait("gpsimd", sY[par], ystore[par])
            t_x = p.dma("gpsimd", sRs, xmt[:, par, :],
                        xmoe[cl * D + et * 128:cl * D + (et + 1) * 128, :])
            for om in range(PD):
                ps = psC5 if om % 2 == 0 else psD5
                p.wait("tensor", sV, sV.n)
                p.op("tensor", lambda En, et=et, om=om, ps=ps: En.transpose(
                    out=ps[:, 0:128], in_=finalT[:, om, et * 128:(et + 1) * 128],
                    identity=ident_f[:, :]), sem=sT)
                p.wait("vector", sT, sT.n)
                p.op("vector", lambda En, om=om, ps=ps, par=par, et=et:
                     En.tensor_add(
                         out=ybuf[:, par, om * 128:(om + 1) * 128],
                         in0=ps[:, 0:128],
                         in1=facc[:, et, om * 128:(om + 1) * 128]), sem=sV)
            p.wait("vector", sRs, t_x)
            p.op("vector", lambda En, par=par: En.tensor_add(
                out=ybuf[:, par, :], in0=ybuf[:, par, :], in1=xmt[:, par, :]),
                sem=sV)
            p.wait("scalar", sV, sV.n)
            p.dma("scalar", sY[par],
                  y[cl * D + et * 128: cl * D + (et + 1) * 128, :],
                  ybuf[:, par, :])
            ystore[par] = sY[par].n
        cl_done = sT.n
        cl_done_v = sV.n

    if "dbg_facc" in dbg:
        p.wait("gpsimd", sV, sV.n)
        p.dma("gpsimd", sSt2, dbg["dbg_facc"][:, 8:16, :], facc[:, :, :])

    p.wait("scalar", sY[0], sY[0].n)
    p.wait("scalar", sY[1], sY[1].n)
    p.wait("sync", sY[0], sY[0].n)
    p.wait("sync", sY[1], sY[1].n)
    ps5.close()
    st_p5.close()
    st_tab5.close()
    st_glob.close()

    p.finalize()
    nc.compile()
    return nc


_NC_CACHE = {}


def _get_nc(debug=()):
    key = tuple(sorted(debug))
    if key not in _NC_CACHE:
        _NC_CACHE[key] = build_nc(debug=key)
    return _NC_CACHE[key]


def make_in_maps(inputs):
    x = np.ascontiguousarray(np.asarray(inputs["L0_attn_q"],
                                        dtype=np.float32).reshape(S, D))
    base = {
        "x": x,
        "wq": np.ascontiguousarray(np.asarray(inputs["L0_attn_wq"], np.float32)),
        "wk": np.ascontiguousarray(np.asarray(inputs["L0_attn_wk"], np.float32)),
        "wv": np.ascontiguousarray(np.asarray(inputs["L0_attn_wv"], np.float32)),
        "wo": np.ascontiguousarray(np.asarray(inputs["L0_attn_wo"], np.float32)),
        "cent": np.ascontiguousarray(np.asarray(inputs["L0_ffn_centroids"],
                                                np.float32)),
        "ebias": np.ascontiguousarray(np.asarray(inputs["L0_ffn_expert_biases"],
                                                 np.float32).reshape(1, E)),
        "wsup": np.ascontiguousarray(np.asarray(inputs["L0_ffn_Ws_up"], np.float32)),
        "wsdn": np.ascontiguousarray(np.asarray(inputs["L0_ffn_Ws_down"],
                                                np.float32)),
        "wup": np.ascontiguousarray(np.asarray(inputs["L0_ffn_W_up"], np.float32)),
        "wdn": np.ascontiguousarray(np.asarray(inputs["L0_ffn_W_down"], np.float32)),
        "ident_b_h": np.eye(128, dtype=ml_dtypes.bfloat16),
        "ident_f_h": np.eye(128, dtype=np.float32),
        "ones_b_h": np.ones((1, 128), dtype=ml_dtypes.bfloat16),
        "ones_f_h": np.ones((1, 128), dtype=np.float32),
        "cpool_h": np.tile(np.array(
            [[EPS, 0.125, 1.0, 1e-8, 1.0 / D, -1.0, -1e30, 0.0,
              float(ZROW), float(-ZROW), 2.0, 3.0, 4.0, 5.0, 6.0, 7.0]],
            np.float32), (128, 1)),
        "ecb_h": np.tile((np.arange(E, dtype=np.float32) % EH) * CAP,
                         (128, 1)),
    }
    e_idx = np.arange(D)
    in_maps = []
    for j in range(NCORES):
        m = dict(base)
        m["xq"] = np.ascontiguousarray(x[j * T:(j + 1) * T])
        rows = np.concatenate([e_idx * 16 + (2 * j), e_idx * 16 + (2 * j + 1)])
        m["xmoe"] = np.ascontiguousarray(x[rows])
        in_maps.append(m)
    return in_maps


def assemble(results):
    out = np.empty((S, D), dtype=np.float32)
    e_idx = np.arange(D)
    for j in range(NCORES):
        yj = np.asarray(results[j]["y"]).reshape(2, D, D)
        out[e_idx * 16 + 2 * j] = yj[0]
        out[e_idx * 16 + 2 * j + 1] = yj[1]
    return out.reshape(1, S, D)


def kernel(**inputs):
    nc = _get_nc()
    in_maps = make_in_maps(inputs)
    res = run_bass_kernel_spmd(nc, in_maps, core_ids=list(range(NCORES)))
    return assemble(res.results)
